# version 32
# baseline (speedup 1.0000x reference)
import numpy as np
from contextlib import ExitStack

import concourse.bass as bass
import concourse.mybir as mybir
from concourse.bass_utils import run_bass_kernel_spmd

F32 = mybir.dt.float32
BF16 = mybir.dt.bfloat16
I8 = mybir.dt.int8
AF = mybir.ActivationFunctionType
ALU = mybir.AluOpType
AX = mybir.AxisListType

P = 128
B = 8
S = 4096
D = 1024
HC = D // P
NT = S // P
I = 2816
IC = I // P
L = 2048
FC = 17
W = 512
NW = S // W
INV127 = float(np.float32(1.0) / np.float32(127.0))
EPS = 1e-5

ENGS = ("sync", "vector", "scalar", "tensor", "gpsimd")


class Buf:
    __slots__ = ("last_w", "readers")

    def __init__(self):
        self.last_w = None
        self.readers = []


class Prog:

    def __init__(self):
        self.ops = {e: [] for e in ENGS}
        self.cnt = {e: 0 for e in ENGS}
        self.lane_cnt = {}
        self.lane_ids = {}

    def _lane(self, obj, sw):
        key = (id(obj), sw)
        lid = self.lane_ids.get(key)
        if lid is None:
            lid = f"lane{len(self.lane_ids)}{'S' if sw else 'H'}"
            self.lane_ids[key] = lid
            self.lane_cnt[lid] = 0
        return lid

    def emit(self, eng, fn, reads=(), writes=(), sreads=(), dma=False, lane=None):
        deps = {}

        def add(ev, force=False):
            if ev is not None:
                key, c, was_dma = ev
                if (key == eng == "tensor" and not was_dma and not dma
                        and not force):
                    return
                if deps.get(key, 0) < c:
                    deps[key] = c

        for b in reads:
            add(b.last_w)
        for b in sreads:
            add(b.last_w, force=True)
        for b in writes:
            add(b.last_w)
            for ev in b.readers:
                add(ev)
        if dma:
            lobj = lane if lane is not None else (writes[0] if writes else reads[0])
            lid = self._lane(lobj, eng == "gpsimd")
            self.lane_cnt[lid] += 16
            ev_self = (lid, self.lane_cnt[lid], True)
        else:
            self.cnt[eng] += 1
            ev_self = (eng, self.cnt[eng], False)
        for b in reads:
            b.readers.append(ev_self)
        for b in sreads:
            b.readers.append(ev_self)
        for b in writes:
            b.last_w = ev_self
            b.readers = []
        self.ops[eng].append((deps, fn, ev_self))

    CAP = 240

    def _epochs(self, lid):
        return (self.lane_cnt[lid] - 16) // Prog.CAP + 1

    def replay(self, nc, block, ctx):
        sems = {e: ctx.enter_context(nc.semaphore(f"sem_{e}")) for e in ENGS}
        final = dict(self.cnt)
        for lid, tot in self.lane_cnt.items():
            for k in range(self._epochs(lid)):
                kk = f"{lid}_e{k}"
                sems[kk] = ctx.enter_context(nc.semaphore(kk))
                final[kk] = min(tot - k * Prog.CAP, Prog.CAP)

        def make_body(eng):
            def body(h):
                waited = {}

                def wait(kk, c):
                    if waited.get(kk, 0) < c:
                        h.wait_ge(sems[kk], c)
                        waited[kk] = c

                def lane_wait(key, c):
                    e = (c - 16) // Prog.CAP
                    for k in range(e):
                        wait(f"{key}_e{k}", Prog.CAP)
                    wait(f"{key}_e{e}", c - e * Prog.CAP)

                for deps, fn, ev_self in self.ops[eng]:
                    for key, c in sorted(deps.items()):
                        if key in self.lane_cnt:
                            lane_wait(key, c)
                        else:
                            wait(key, c)
                    inst = fn(h)
                    key, c, was_dma = ev_self
                    if was_dma:
                        e = (c - 16) // Prog.CAP
                        inst.then_inc(sems[f"{key}_e{e}"], 16)
                    else:
                        inst.then_inc(sems[key], 1)
                for key, c in sorted(final.items()):
                    if c and waited.get(key, 0) < c and key != eng:
                        h.wait_ge(sems[key], c)
            return body

        block.sync(make_body("sync"))
        block.vector(make_body("vector"))
        block.scalar(make_body("scalar"))
        block.tensor(make_body("tensor"))
        block.gpsimd(make_body("gpsimd"))



def build_l1(debug=False):
    nc = bass.Bass("TRN2", target_bir_lowering=False)
    x_d = nc.dram_tensor("x_in", [S, D], F32, kind="ExternalInput").ap()
    f_d = nc.dram_tensor("f_in", [FC, P, P], F32, kind="ExternalInput").ap()
    xres_d = nc.dram_tensor("xres", [S, D], F32, kind="ExternalOutput").ap()
    amax_d = nc.dram_tensor("amax1", [1, 1], F32, kind="ExternalOutput").ap()
    if debug:
        ydbg_d = nc.dram_tensor("ydbg", [2 * P, D], F32, kind="ExternalOutput").ap()
        sdbg_d = nc.dram_tensor("sdbg", [2 * P, 20], F32, kind="ExternalOutput").ap()
    dbg_sb = None
    bounce = nc.dram_tensor("bounce1", [1, P], F32)
    eye_h = nc.inline_tensor(np.eye(P, dtype=np.float32), "eye_f32_h")
    RING = 32

    with ExitStack() as ctx:
        sb = lambda n, s, d: ctx.enter_context(nc.sbuf_tensor(n, s, d))
        ps = lambda n, s, d: ctx.enter_context(nc.psum_tensor(n, s, d))
        NBUF = 4
        xf = [sb(f"xf{i}", [P, D], F32) for i in range(2)]
        ring = sb("ring", [P, RING * D], BF16)
        Ff = sb("Ff", [P, FC * P], F32)
        Fb = sb("Fb", [P, FC * P], BF16)
        eye = sb("eye", [P, P], F32)
        xrt = [sb(f"xrt{i}", [P, D], F32) for i in range(NBUF)]
        sqscr = [sb(f"sqscr{i}", [P, W], F32) for i in range(2)]
        stats = [sb(f"stats{i}", [P, 12], F32) for i in range(NBUF)]
        mvA = [sb(f"mvA{i}", [P, 2], F32) for i in range(NBUF)]
        mvB = [sb(f"mvB{i}", [P, 2], F32) for i in range(NBUF)]
        mv = [sb(f"mv{i}", [P, 2], F32) for i in range(NBUF)]
        dmq = [sb(f"dmq{i}", [P, 1], F32) for i in range(NBUF)]
        dm2 = [sb(f"dm2_{i}", [P, 1], F32) for i in range(NBUF)]
        vs = [sb(f"vs{i}", [P, 1], F32) for i in range(NBUF)]
        sdv = [sb(f"sdv{i}", [P, 1], F32) for i in range(NBUF)]
        rstd = [sb(f"rstd{i}", [P, 1], F32) for i in range(NBUF)]
        nb = [sb(f"nb{i}", [P, 1], F32) for i in range(NBUF)]
        slots = sb("slots", [P, NT], F32)
        epsc = sb("epsc", [P, 1], F32)
        acol = sb("acol", [P, 1], F32)
        arow = sb("arow", [1, P], F32)
        aout = sb("aout", [1, 1], F32)
        y = [ps(f"y{i}", [P, W], F32) for i in range(2 * NBUF)]
        if debug:
            dbg_sb = sb("dbg_sb", [P, D], F32)
        bdbg = Buf()
        pr = Prog()
        bF = Buf(); bFb = Buf(); beye = Buf()
        bxf = [Buf() for _ in range(2)]
        bring = [Buf() for _ in range(RING)]
        bxrt = [Buf() for _ in range(NBUF)]
        bsqscr = [Buf() for _ in range(2)]
        bstats = [Buf() for _ in range(NBUF)]
        bmvA = [Buf() for _ in range(NBUF)]
        bmvB = [Buf() for _ in range(NBUF)]
        bmv = [Buf() for _ in range(NBUF)]
        bdmq = [Buf() for _ in range(NBUF)]
        bdm2 = [Buf() for _ in range(NBUF)]
        bvs = [Buf() for _ in range(NBUF)]
        bsdv = [Buf() for _ in range(NBUF)]
        brstd = [Buf() for _ in range(NBUF)]
        bnb = [Buf() for _ in range(NBUF)]
        by = [Buf() for _ in range(2 * NBUF)]
        bslot = [Buf() for _ in range(NT)]
        bacol = Buf(); bbounce = Buf(); barow = Buf(); baout = Buf()
        bepsc = Buf()
        bxres_d = [Buf() for _ in range(NT)]
        bamax_d = Buf()

        pr.emit("gpsimd", lambda h: h.memset(epsc[:], EPS), writes=[bepsc])
        pr.emit("sync", lambda h: h.dma_start(out=Ff[:].rearrange("p (c i) -> p c i", c=FC), in_=f_d.transpose([1, 0, 2])), writes=[bF], dma=True)
        pr.emit("sync", lambda h: h.dma_start(out=eye[:], in_=eye_h.ap()), writes=[beye], dma=True)
        pr.emit("vector", lambda h: h.tensor_add(out=Ff[:, 0:P], in0=Ff[:, 0:P], in1=eye[:]), reads=[beye], writes=[bF])
        pr.emit("vector", lambda h: h.tensor_copy(out=Fb[:], in_=Ff[:]), reads=[bF], writes=[bFb])

        xfq = [sb(f"xfq{i}", [P, D], F32) for i in range(2)]
        bxfq = [Buf() for _ in range(2)]

        def emit_load(k):
            q = "sync" if k % 2 == 0 else "scalar"
            bx = bxf[k % 2] if k % 2 == 0 else bxfq[k % 4 // 2]
            xt = xf[k % 2] if k % 2 == 0 else xfq[k % 4 // 2]
            def ld(h, k=k, xt=xt):
                return h.dma_start(out=xt[:], in_=x_d[k * P:(k + 1) * P, :])
            pr.emit(q, ld, reads=[], writes=[bx], dma=True)
            sl = k
            def cast(h, xt=xt, sl=sl):
                return h.copy(out=ring[:, sl * D:(sl + 1) * D], in_=xt[:])
            pr.emit("scalar", cast, reads=[bx], writes=[bring[sl]])

        order = []
        lo, hi = 0, NT - 1
        while lo <= hi:
            order.append(lo)
            if hi != lo:
                order.append(hi)
            lo += 1
            hi -= 1

        load_order = []
        seen = set()
        for T in order:
            cmax = min(FC - 1, NT - 1 - T)
            for c in range(cmax + 1):
                k = T + c
                if k not in seen:
                    seen.add(k)
                    load_order.append(k)
        for k in load_order[:17]:
            emit_load(k)
        rest = load_order[17:]

        def stage_C2(T, par):
            def amx(h, par=par, T=T):
                return h.reduce_max(out=slots[:, T:T + 1], in_=xrt[par][:], axis=AX.X, apply_absolute_value=True)
            pr.emit("vector", amx, reads=[bxrt[par]], writes=[bslot[T]])
            def st(h, par=par, T=T):
                return h.dma_start(out=xres_d[T * P:(T + 1) * P, :], in_=xrt[par][:])
            pr.emit("scalar", st, reads=[bxrt[par]], writes=[bxres_d[T]], dma=True, lane=bxrt[par])

        prev = None
        for ti, T in enumerate(order):
            par = ti % NBUF
            if ti < len(rest):
                emit_load(rest[ti])
            cmax = min(FC - 1, NT - 1 - T)
            for hh in range(2):
                yb = y[par * 2 + hh]
                byb = by[par * 2 + hh]
                for c in range(cmax + 1):
                    sl = (T + c) % RING
                    def mm(h, yb=yb, c=c, sl=sl, hh=hh, cmax=cmax):
                        return h.matmul(out=yb[:], lhsT=Fb[:, c * P:(c + 1) * P],
                                        rhs=ring[:, sl * D + hh * W: sl * D + hh * W + W],
                                        start=(c == 0), stop=(c == cmax))
                    pr.emit("tensor", mm, reads=[bFb, bring[sl]], writes=[byb])
            for hh in range(2):
                def rs(h, par=par, hh=hh):
                    return h.reduce_sum(out=stats[par][:, hh:hh + 1], in_=y[par * 2 + hh][:], axis=AX.X)
                pr.emit("vector", rs, reads=[by[par * 2 + hh]], writes=[bstats[par]])
                def sqa(h, par=par, hh=hh):
                    return h.activation(out=sqscr[par % 2][:], in_=y[par * 2 + hh][:], func=AF.Square, accum_out=stats[par][:, 2 + hh:3 + hh])
                pr.emit("scalar", sqa, reads=[by[par * 2 + hh]], writes=[bstats[par], bsqscr[par % 2]])
            pr.emit("vector", lambda h, par=par: h.tensor_scalar(out=mv[par][:, 0:1], in0=stats[par][:, 0:1], scalar1=stats[par][:, 1:2], scalar2=1.0 / D, op0=ALU.add, op1=ALU.mult), reads=[bstats[par]], sreads=[bstats[par]], writes=[bmv[par]])
            pr.emit("vector", lambda h, par=par: h.tensor_scalar(out=dmq[par][:], in0=stats[par][:, 2:3], scalar1=stats[par][:, 3:4], scalar2=1.0 / D, op0=ALU.add, op1=ALU.mult), reads=[bstats[par]], sreads=[bstats[par]], writes=[bdmq[par]])
            pr.emit("vector", lambda h, par=par: h.tensor_scalar(out=dm2[par][:], in0=mv[par][:, 0:1], scalar1=mv[par][:, 0:1], scalar2=None, op0=ALU.mult), reads=[bmv[par]], sreads=[bmv[par]], writes=[bdm2[par]])
            pr.emit("vector", lambda h, par=par: h.tensor_sub(out=mv[par][:, 1:2], in0=dmq[par][:], in1=dm2[par][:]), reads=[bdmq[par], bdm2[par]], writes=[bmv[par]])
            def sq(h, par=par):
                return h.activation(out=sdv[par][:], in_=mv[par][:, 1:2], func=AF.Sqrt, bias=epsc[:])
            pr.emit("scalar", sq, reads=[bmv[par], bepsc], writes=[bsdv[par]])
            def rc(h, par=par):
                return h.reciprocal(out=rstd[par][:], in_=sdv[par][:])
            pr.emit("vector", rc, reads=[bsdv[par]], writes=[brstd[par]])
            pr.emit("gpsimd", lambda h, par=par: h.tensor_scalar(out=nb[par][:], in0=mv[par][:, 0:1], scalar1=rstd[par][:], scalar2=-1.0, op0=ALU.mult, op1=ALU.mult), reads=[bmv[par]], sreads=[brstd[par]], writes=[bnb[par]])
            for hh in range(2):
                def norm(h, par=par, hh=hh):
                    return h.activation(out=xrt[par][:, hh * W:(hh + 1) * W], in_=y[par * 2 + hh][:],
                                        func=AF.Identity, scale=rstd[par][:], bias=nb[par][:])
                pr.emit("scalar", norm, reads=[by[par * 2 + hh], brstd[par], bnb[par]], writes=[bxrt[par]])
            if prev is not None:
                stage_C2(*prev)
            prev = (T, par)
        stage_C2(*prev)

        pr.emit("vector", lambda h: h.reduce_max(out=acol[:], in_=slots[:], axis=AX.X), reads=bslot, writes=[bacol])
        pr.emit("scalar", lambda h: h.dma_start(out=bounce.ap().transpose([1, 0]), in_=acol[:]), reads=[bacol], writes=[bbounce], dma=True)
        pr.emit("sync", lambda h: h.dma_start(out=arow[:], in_=bounce.ap()), reads=[bbounce], writes=[barow], dma=True)
        pr.emit("vector", lambda h: h.reduce_max(out=aout[:], in_=arow[:], axis=AX.X), reads=[barow], writes=[baout])
        pr.emit("scalar", lambda h: h.dma_start(out=amax_d, in_=aout[:]), reads=[baout], writes=[bamax_d], dma=True)

        block = ctx.enter_context(nc.Block())
        pr.replay(nc, block, ctx)
    return nc



def build_l2(debug=False):
    nc = bass.Bass("TRN2", target_bir_lowering=False)
    xres_d = nc.dram_tensor("xres_in", [S, D], F32, kind="ExternalInput").ap()
    am_d = nc.dram_tensor("amax1_all", [1, B], F32, kind="ExternalInput").ap()
    clip_d = nc.dram_tensor("clip_x_in", [1, 1], F32, kind="ExternalInput").ap()
    wgn_d = nc.dram_tensor("wg_nat", [IC, P, D], F32, kind="ExternalInput").ap()
    wun_d = nc.dram_tensor("wu_nat", [IC, P, D], F32, kind="ExternalInput").ap()
    wdn_d = nc.dram_tensor("wd_nat", [HC, P, I], F32, kind="ExternalInput").ap()
    wdt_d = nc.dram_tensor("wd_t", [IC, P, D], F32, kind="ExternalInput").ap()

    g_d = nc.dram_tensor("g_out", [NT, P, IC, P], F32, kind="ExternalOutput").ap()
    mdt_d = nc.dram_tensor("mdt_out", [IC, P, D], BF16, kind="ExternalOutput").ap()
    sd_d = nc.dram_tensor("sd_out", [P, HC], F32, kind="ExternalOutput").ap()
    amax2_d = nc.dram_tensor("amax2", [1, 1], F32, kind="ExternalOutput").ap()
    bounce = nc.dram_tensor("bounce2", [1, P], F32)
    brow = nc.dram_tensor("brow", [1, P], F32)
    brow2 = nc.dram_tensor("brow2", [1, P], F32)
    eye_h = nc.inline_tensor(np.eye(P, dtype=np.float32), "eye2_f32_h")
    if debug:
        scdbg_d = nc.dram_tensor("scdbg", [P, 8 + 2 * IC], F32, kind="ExternalOutput").ap()
        xqdbg_d = nc.dram_tensor("xqdbg", [P, D], F32, kind="ExternalOutput").ap()

    with ExitStack() as ctx:
        sb = lambda n, s, d: ctx.enter_context(nc.sbuf_tensor(n, s, d))
        ps = lambda n, s, d: ctx.enter_context(nc.psum_tensor(n, s, d))
        xq = sb("xq", [P, HC * S], BF16)
        mgT = sb("mgT", [P, IC * D], BF16)
        muT = sb("muT", [P, IC * D], BF16)
        wnat = [sb(f"wnat{i}", [P, D], F32) for i in range(2)]
        wT = [sb(f"wT{i}", [P, D], F32) for i in range(2)]
        wcod = [sb(f"wcod{i}", [P, D], I8) for i in range(2)]
        cnat = [sb(f"cnat{i}", [P, D], BF16) for i in range(2)]
        ctm2 = [sb(f"ctm2_{i}", [P, D], BF16) for i in range(2)]
        Rd = sb("Rd", [P, D], F32)

        xrt = [sb(f"xrt{i}", [P, D], F32) for i in range(2)]
        xcod = [sb(f"xcod{i}", [P, D], I8) for i in range(2)]
        ctm = [sb(f"ctm{i}", [P, D], BF16) for i in range(2)]
        tsilu = [sb(f"tsilu{i}", [P, W], F32) for i in range(2)]
        gsb = [sb(f"gsb{i}", [P, W], F32) for i in range(2)]
        a_tmp = [sb(f"atmp{i}", [P, 1], F32) for i in range(2)]
        rcol = [sb(f"rcol{i}", [P, 1], F32) for i in range(2)]
        rcolg = sb("rcolg", [P, 1], F32)
        rcolu = sb("rcolu", [P, 1], F32)
        s_g = sb("s_g", [P, IC], F32)
        s_u = sb("s_u", [P, IC], F32)
        sgc = sb("sgc", [P, IC], F32)
        suc = sb("suc", [P, IC], F32)
        s_d = sb("s_d", [P, HC], F32)
        rd_acc = [sb(f"rdacc{i}", [P, 1], F32) for i in range(2)]
        slots2 = sb("slots2", [P, IC * NW], F32)
        eye = sb("eye2", [P, P], F32)
        eyeb = sb("eyeb", [P, P], BF16)
        ones = sb("ones", [P, P], F32)
        amrow = sb("amrow", [1, B], F32)
        amg = sb("amg", [1, 1], F32)
        clip = sb("clip", [1, 1], F32)
        sig = sb("sig", [1, 1], F32)
        sx = sb("sx", [1, 1], F32)
        rx = sb("rx", [1, 1], F32)
        rxbc = sb("rxbc", [P, 1], F32)
        sxbc = sb("sxbc", [P, 1], F32)
        acol = sb("acol2", [P, 1], F32)
        arow = sb("arow2", [1, P], F32)
        aout = sb("aout2", [1, 1], F32)
        gps = [ps(f"gps{i}", [P, W], F32) for i in range(3)]
        ups = [ps(f"ups{i}", [P, W], F32) for i in range(3)]
        pT = ps("pT", [P, D], BF16)
        pT2 = ps("pT2", [P, D], BF16)
        pr = Prog()
        bxq = [Buf() for _ in range(NT)]
        bmgT = [Buf() for _ in range(IC)]
        bmuT = [Buf() for _ in range(IC)]
        bwnat = [Buf() for _ in range(2)]
        bwT = [Buf() for _ in range(2)]
        bwcod = [Buf() for _ in range(2)]
        bcnat = [Buf() for _ in range(2)]
        bctm2 = [Buf() for _ in range(2)]
        bRd = Buf()
        bxrt = [Buf() for _ in range(2)]
        bxcod = [Buf() for _ in range(2)]
        bctm = [Buf() for _ in range(2)]
        btsilu = [Buf() for _ in range(2)]
        bgsb = [Buf() for _ in range(2)]
        ba_tmp = [Buf() for _ in range(2)]
        brcol = [Buf() for _ in range(2)]
        brcolg = Buf()
        brcolu = Buf()
        bs_g = [Buf() for _ in range(IC)]
        bs_u = [Buf() for _ in range(IC)]
        bsgc = [Buf() for _ in range(IC)]
        bsuc = [Buf() for _ in range(IC)]
        bs_d = [Buf() for _ in range(HC)]
        brd_acc = [Buf() for _ in range(2)]
        bslot2 = [Buf() for _ in range(IC * NW)]
        beye = Buf(); beyeb = Buf(); bones = Buf()
        bamrow = Buf(); bamg = Buf(); bclip = Buf(); bsig = Buf()
        bbrow = Buf(); bbrow2 = Buf()
        bsx = Buf(); brx = Buf(); brxbc = Buf(); bsxbc = Buf()
        bgps = [Buf() for _ in range(3)]
        bups = [Buf() for _ in range(3)]
        bpT = [Buf() for _ in range(HC)]
        _pT2bufs = [Buf() for _ in range(HC)]
        bpT2g = [_pT2bufs, _pT2bufs]
        bpM = Buf()
        bacol = Buf(); bbounce = Buf(); barow = Buf(); baout = Buf()
        bg_d = [Buf() for _ in range(NT)]
        bmdt_d = [Buf() for _ in range(IC)]
        bsd_d = Buf(); bamax2_d = Buf()

        E = pr.emit

        E("sync", lambda h: h.dma_start(out=eye[:], in_=eye_h.ap()), writes=[beye], dma=True)
        E("vector", lambda h: h.tensor_copy(out=eyeb[:], in_=eye[:]), reads=[beye], writes=[beyeb])
        E("gpsimd", lambda h: h.memset(ones[:], 1.0), writes=[bones])

        E("sync", lambda h: h.dma_start(out=amrow[:], in_=am_d), writes=[bamrow], dma=True)
        E("vector", lambda h: h.reduce_max(out=amg[:], in_=amrow[:], axis=AX.X), reads=[bamrow], writes=[bamg])
        E("sync", lambda h: h.dma_start(out=clip[:], in_=clip_d), writes=[bclip], dma=True)
        E("scalar", lambda h: h.activation(out=sig[:], in_=clip[:], func=AF.Sigmoid), reads=[bclip], writes=[bsig])
        E("vector", lambda h: h.tensor_scalar(out=sx[:], in0=sig[:], scalar1=amg[:], scalar2=INV127, op0=ALU.mult, op1=ALU.mult), reads=[bsig], sreads=[bamg], writes=[bsx])
        E("vector", lambda h: h.reciprocal(out=rx[:], in_=sx[:]), reads=[bsx], writes=[brx])
        E("tensor", lambda h: h.matmul(out=gps[0][:, 0:1], lhsT=ones[0:1, :], rhs=rx[:], start=True, stop=True), reads=[bones, brx], writes=[bgps[0]])
        E("vector", lambda h: h.tensor_copy(out=rxbc[:], in_=gps[0][:, 0:1]), reads=[bgps[0]], writes=[brxbc])
        E("tensor", lambda h: h.matmul(out=gps[0][:, 1:2], lhsT=ones[0:1, :], rhs=sx[:], start=True, stop=True), reads=[bones, bsx], writes=[bgps[0]])
        E("vector", lambda h: h.tensor_copy(out=sxbc[:], in_=gps[0][:, 1:2]), reads=[bgps[0]], writes=[bsxbc])

        if debug:
            scdbg = sb("scdbg_sb", [P, 8 + 2 * IC], F32)
            bscdbg = Buf()

        def emit_xq_tile(t):
            j = t % 2
            E("sync", lambda h, t=t, j=j: h.dma_start(out=xrt[j][:], in_=xres_d[t * P:(t + 1) * P, :]), writes=[bxrt[j]], dma=True)
            E("scalar", lambda h, j=j: h.activation(out=xcod[j][:], in_=xrt[j][:], func=AF.Copy, scale=rxbc[:]), reads=[bxrt[j], brxbc], writes=[bxcod[j]])
            E("scalar", lambda h, j=j: h.copy(out=ctm[j][:], in_=xcod[j][:]), reads=[bxcod[j]], writes=[bctm[j]])
            for hc in range(HC):
                E("tensor", lambda h, j=j, hc=hc: h.transpose(out=pT[:, hc * P:(hc + 1) * P], in_=ctm[j][:, hc * P:(hc + 1) * P], identity=eyeb[:]), reads=[bctm[j], beyeb], writes=[bpT[hc]])
            def evac(h, t=t):
                dst = xq[:].rearrange("p (hc s) -> p hc s", hc=HC)[:, :, t * P:(t + 1) * P]
                src = pT[:].rearrange("p (hc f) -> p hc f", hc=HC)
                return h.tensor_copy(out=dst, in_=src)
            E("vector", evac, reads=bpT, writes=[bxq[t]])

        def _w_sel(which):
            return ((wgn_d, mgT, bmgT, s_g, bs_g, sgc, bsgc) if which == "g"
                    else (wun_d, muT, bmuT, s_u, bs_u, suc, bsuc))

        def w_chunk_s1(which, ic):
            j = 0 if which == "g" else 1
            nat_d, dst, bdst, scol, bscol, ccol, bccol = _w_sel(which)
            E("sync", lambda h, ic=ic, j=j, nat_d=nat_d: h.dma_start(out=wnat[j][:], in_=nat_d[ic]), writes=[bwnat[j]], dma=True)
            E("vector", lambda h, j=j: h.reduce_max(out=a_tmp[j][:], in_=wnat[j][:], axis=AX.X, apply_absolute_value=True), reads=[bwnat[j]], writes=[ba_tmp[j]])
            E("vector", lambda h, j=j, scol=scol, ic=ic: h.tensor_scalar_mul(out=scol[:, ic:ic + 1], in0=a_tmp[j][:], scalar1=INV127), reads=[ba_tmp[j]], writes=[bscol[ic]])
            E("vector", lambda h, scol=scol, ic=ic, ccol=ccol: h.tensor_mul(out=ccol[:, ic:ic + 1], in0=scol[:, ic:ic + 1], in1=sxbc[:]), reads=[bscol[ic], bsxbc], writes=[bccol[ic]])
            E("vector", lambda h, j=j, wr=(brcolg if which == "g" else brcolu), rt=(rcolg if which == "g" else rcolu), scol=scol, ic=ic: h.reciprocal(out=rt[:], in_=scol[:, ic:ic + 1]), reads=[bscol[ic]], writes=[(brcolg if which == "g" else brcolu)])

        def w_chunk_s2a(which, ic):
            j = 0 if which == "g" else 1
            rt, brt = (rcolg, brcolg) if which == "g" else (rcolu, brcolu)
            cj = 0 if which == "g" else 1
            E("scalar", lambda h, j=j, rt=rt, brt=brt, cj=cj: h.activation(out=wcod[cj][:], in_=wnat[j][:], func=AF.Copy, scale=rt[:]), reads=[bwnat[j], brt], writes=[bwcod[cj]])
            E("scalar", lambda h, cj=cj: h.copy(out=cnat[cj][:], in_=wcod[cj][:]), reads=[bwcod[cj]], writes=[bcnat[cj]])

        def w_chunk_s2b(which, ic):
            cj = 0 if which == "g" else 1
            for hc in range(HC):
                E("tensor", lambda h, cj=cj, hc=hc: h.transpose(out=pT2[:, hc * P:(hc + 1) * P], in_=cnat[cj][:, hc * P:(hc + 1) * P], identity=eyeb[:]), reads=[bcnat[cj], beyeb], writes=[bpT2g[cj][hc]])

        def w_chunk_s3(which, ic):
            nat_d, dst, bdst, scol, bscol, ccol, bccol = _w_sel(which)
            cj = 0 if which == "g" else 1
            def evac(h, dst=dst, ic=ic, cj=cj):
                return h.copy(out=dst[:, ic * D:(ic + 1) * D], in_=pT2[:])
            E("scalar", evac, reads=bpT2g[cj], writes=[bdst[ic]])

        def wd_nat_piece(hc, k):
            pieces = [(0, D), (D, D), (2 * D, I - 2 * D)]
            off, ln = pieces[k]
            j = k % 2
            E("sync", lambda h, hc=hc, off=off, ln=ln: h.dma_start(out=xrt[1][:, 0:ln], in_=wdn_d[hc, :, off:off + ln]), writes=[bxrt[1]], dma=True)
            E("vector", lambda h, j=j, ln=ln: h.reduce_max(out=a_tmp[j][:], in_=xrt[1][:, 0:ln], axis=AX.X, apply_absolute_value=True), reads=[bxrt[1]], writes=[ba_tmp[j]])
            if k == 0:
                E("vector", lambda h: h.tensor_copy(out=rd_acc[0][:], in_=a_tmp[0][:]), reads=[ba_tmp[0]], writes=[brd_acc[0]])
            else:
                E("vector", lambda h, k=k, j=j: h.tensor_max(out=rd_acc[k % 2][:], in0=rd_acc[(k - 1) % 2][:], in1=a_tmp[j][:]), reads=[brd_acc[(k - 1) % 2], ba_tmp[j]], writes=[brd_acc[k % 2]])

        def wd_nat_finish(hc):
            E("vector", lambda h, hc=hc: h.tensor_scalar_mul(out=s_d[:, hc:hc + 1], in0=rd_acc[0][:], scalar1=INV127), reads=[brd_acc[0]], writes=[bs_d[hc]])
            j = hc % 2
            E("vector", lambda h, j=j, hc=hc: h.reciprocal(out=rcol[j][:], in_=s_d[:, hc:hc + 1]), reads=[bs_d[hc]], writes=[brcol[j]])
            E("scalar", lambda h, j=j: h.dma_start(out=brow.ap().transpose([1, 0]), in_=rcol[j][:]), reads=[brcol[j]], writes=[bbrow], dma=True, lane=bbrow)
            def ldRd(h, hc=hc):
                return h.dma_start(out=Rd[:, hc * P:(hc + 1) * P], in_=brow.ap().partition_broadcast(P))
            E("gpsimd", ldRd, reads=[bbrow], writes=[bRd], dma=True, lane=bRd)
            E("scalar", lambda h, hc=hc: h.dma_start(out=brow2.ap().transpose([1, 0]), in_=s_d[:, hc:hc + 1]), reads=[bs_d[hc]], writes=[bbrow2], dma=True, lane=bbrow2)
            def ldSd(h, hc=hc):
                return h.dma_start(out=xrt[0][:, hc * P:(hc + 1) * P], in_=brow2.ap().partition_broadcast(P))
            E("gpsimd", ldSd, reads=[bbrow2], writes=[bxrt[0]], dma=True, lane=bxrt[0])

        def emit_wd_t_load(icc):
            j = icc % 2
            E("sync", lambda h, icc=icc, j=j: h.dma_start(out=wT[j][:], in_=wdt_d[icc]), writes=[bwT[j]], dma=True)

        def emit_wd_t_compute(icc):
            j = icc % 2
            E("vector", lambda h, j=j: h.tensor_mul(out=wcod[j][:], in0=wT[j][:], in1=Rd[:]), reads=[bwT[j], bRd], writes=[bwcod[j]])
            E("gpsimd", lambda h, j=j: h.tensor_mul(out=ctm2[j][:], in0=wcod[j][:], in1=xrt[0][:]), reads=[bwcod[j], bxrt[0]], writes=[bctm2[j]])
            E("gpsimd", lambda h, j=j, icc=icc: h.dma_start(out=mdt_d[icc], in_=ctm2[j][:]), reads=[bctm2[j]], writes=[bmdt_d[icc]], dma=True, lane=bctm2[j])

        def emit_sweep(ic, w):
            par = (ic * NW + w) % 3
            sx2 = (ic * NW + w) % 2
            for hc in range(HC):
                def mmg(h, ic=ic, w=w, hc=hc, par=par):
                    return h.matmul(out=gps[par][:], lhsT=mgT[:, ic * D + hc * P: ic * D + (hc + 1) * P],
                                    rhs=xq[:, hc * S + w * W: hc * S + (w + 1) * W],
                                    start=(hc == 0), stop=(hc == HC - 1))
                pr.emit("tensor", mmg, reads=[bmgT[ic]] + bxq[4 * w:4 * w + 4], writes=[bgps[par]])
            for hc in range(HC):
                def mmu(h, ic=ic, w=w, hc=hc, par=par):
                    return h.matmul(out=ups[par][:], lhsT=muT[:, ic * D + hc * P: ic * D + (hc + 1) * P],
                                    rhs=xq[:, hc * S + w * W: hc * S + (w + 1) * W],
                                    start=(hc == 0), stop=(hc == HC - 1))
                pr.emit("tensor", mmu, reads=[bmuT[ic]] + bxq[4 * w:4 * w + 4], writes=[bups[par]])
            E("scalar", lambda h, par=par, sx2=sx2, ic=ic: h.activation(out=tsilu[sx2][:], in_=gps[par][:], func=AF.Silu, scale=sgc[:, ic:ic + 1]), reads=[bgps[par], bsgc[ic]], writes=[btsilu[sx2]])
            E("vector", lambda h, par=par, sx2=sx2, ic=ic: h.scalar_tensor_tensor(out=gsb[sx2][:], in0=ups[par][:], scalar=suc[:, ic:ic + 1], in1=tsilu[sx2][:], op0=ALU.mult, op1=ALU.mult), reads=[bups[par], btsilu[sx2]], sreads=[bsuc[ic]], writes=[bgsb[sx2]])
            E("vector", lambda h, sx2=sx2, ic=ic, w=w: h.reduce_max(out=slots2[:, ic * NW + w: ic * NW + w + 1], in_=gsb[sx2][:], axis=AX.X, apply_absolute_value=True), reads=[bgsb[sx2]], writes=[bslot2[ic * NW + w]])
            def stg(h, sx2=sx2, ic=ic, w=w):
                dst = g_d[4 * w:4 * w + 4, :, ic, :].transpose([1, 0, 2])
                src = gsb[sx2][:].rearrange("p (tt f) -> p tt f", tt=4)
                return h.dma_start(out=dst, in_=src)
            E("gpsimd", stg, reads=[bgsb[sx2]], writes=[bg_d[4 * w + k] for k in range(4)], dma=True, lane=bgsb[sx2])

        for t in range(8):
            emit_xq_tile(t)
        for which, ic in (("g", 0), ("u", 0), ("g", 1), ("u", 1), ("g", 2), ("u", 2)):
            w_chunk_s1(which, ic)
            w_chunk_s2a(which, ic)
            w_chunk_s2b(which, ic)
            w_chunk_s3(which, ic)
        down_nat_at = {2 + k: k for k in range(HC)}
        wdt_load_q = list(range(IC))
        wdt_comp_q = list(range(IC))
        wdt_inflight = []
        for ic in range(IC):
            for w in range(NW):
                if ic + 3 < IC:
                    if w == 1:
                        w_chunk_s2a("g", ic + 3)
                    elif w == 2:
                        w_chunk_s2a("u", ic + 3)
                    elif w == 3:
                        w_chunk_s2b("g", ic + 3)
                    elif w == 4:
                        w_chunk_s3("g", ic + 3)
                    elif w == 5:
                        w_chunk_s2b("u", ic + 3)
                    elif w == 6:
                        w_chunk_s3("u", ic + 3)
                emit_sweep(ic, w)
                if ic == 0 and w < NW - 2:
                    for t in range(4 * (w + 2), 4 * (w + 3)):
                        emit_xq_tile(t)
                if ic + 3 < IC and w == 0:
                    w_chunk_s1("g", ic + 3)
                    w_chunk_s1("u", ic + 3)
                if ic in down_nat_at:
                    if w in (1, 3, 5):
                        wd_nat_piece(down_nat_at[ic], (w - 1) // 2)
                    elif w == 7:
                        wd_nat_finish(down_nat_at[ic])
                if ic >= 10:
                    if w % 2 == 0 and wdt_load_q and len(wdt_inflight) < 2:
                        icc = wdt_load_q.pop(0)
                        emit_wd_t_load(icc)
                        wdt_inflight.append(icc)
                    elif w % 2 == 1 and (len(wdt_inflight) == 2 or not wdt_load_q) and wdt_inflight:
                        icc = wdt_inflight.pop(0)
                        assert icc == wdt_comp_q.pop(0)
                        emit_wd_t_compute(icc)
        E("scalar", lambda h: h.dma_start(out=sd_d, in_=s_d[:]), reads=bs_d, writes=[bsd_d], dma=True)
        E("vector", lambda h: h.reduce_max(out=acol[:], in_=slots2[:], axis=AX.X), reads=bslot2, writes=[bacol])
        E("scalar", lambda h: h.dma_start(out=bounce.ap().transpose([1, 0]), in_=acol[:]), reads=[bacol], writes=[bbounce], dma=True)
        E("sync", lambda h: h.dma_start(out=arow[:], in_=bounce.ap()), reads=[bbounce], writes=[barow], dma=True)
        E("vector", lambda h: h.reduce_max(out=aout[:], in_=arow[:], axis=AX.X), reads=[barow], writes=[baout])
        E("scalar", lambda h: h.dma_start(out=amax2_d, in_=aout[:]), reads=[baout], writes=[bamax2_d], dma=True)

        block = ctx.enter_context(nc.Block())
        pr.replay(nc, block, ctx)
    return nc



def build_l3():
    nc = bass.Bass("TRN2", target_bir_lowering=False)
    g_d = nc.dram_tensor("g_in", [NT, P, IC, P], F32, kind="ExternalInput").ap()
    xres_d = nc.dram_tensor("xres3_in", [S, D], F32, kind="ExternalInput").ap()
    mdt_d = nc.dram_tensor("mdt_in", [IC, P, D], BF16, kind="ExternalInput").ap()
    am_d = nc.dram_tensor("amax2_all", [1, B], F32, kind="ExternalInput").ap()
    clip_d = nc.dram_tensor("clip_down_in", [1, 1], F32, kind="ExternalInput").ap()
    out_d = nc.dram_tensor("out", [S, D], F32, kind="ExternalOutput").ap()
    eye_h = nc.inline_tensor(np.eye(P, dtype=np.float32), "eye3_f32_h")

    with ExitStack() as ctx:
        sb = lambda n, s, d: ctx.enter_context(nc.sbuf_tensor(n, s, d))
        ps = lambda n, s, d: ctx.enter_context(nc.psum_tensor(n, s, d))
        NB3 = 3
        NG = 4
        NX = 6
        mdT = sb("mdT", [P, IC * D], BF16)
        gin = [sb(f"gin{i}", [P, I], F32) for i in range(NG)]
        n2i = [sb(f"n2i{i}", [P, I], I8) for i in range(NB3)]
        n2b = [sb(f"n2b{i}", [P, I], BF16) for i in range(NB3)]
        xrt = [sb(f"xrt3{i}", [P, D], F32) for i in range(NX)]
        tmp = [sb(f"tmp3{i}", [P, W], F32) for i in range(NB3)]
        y2 = [sb(f"y2_{i}", [P, D], F32) for i in range(NB3)]
        outt = [sb(f"outt{i}", [P, D], F32) for i in range(NB3)]
        stats = [sb(f"st3{i}", [P, 12], F32) for i in range(NB3)]
        mvA = [sb(f"mv3A{i}", [P, 2], F32) for i in range(NB3)]
        mvB = [sb(f"mv3B{i}", [P, 2], F32) for i in range(NB3)]
        mv = [sb(f"mv3{i}", [P, 2], F32) for i in range(NB3)]
        dmq = [sb(f"dmq3{i}", [P, 1], F32) for i in range(NB3)]
        dm2 = [sb(f"dm23_{i}", [P, 1], F32) for i in range(NB3)]
        vs = [sb(f"vs3{i}", [P, 1], F32) for i in range(NB3)]
        sdv = [sb(f"sdv3{i}", [P, 1], F32) for i in range(NB3)]
        rstd = [sb(f"rstd3{i}", [P, 1], F32) for i in range(NB3)]
        nb = [sb(f"nb3{i}", [P, 1], F32) for i in range(NB3)]
        epsc = sb("epsc3", [P, 1], F32)
        eye = sb("eye3", [P, P], F32)
        ones = sb("ones3", [P, P], F32)
        amrow = sb("amrow3", [1, B], F32)
        amg = sb("amg3", [1, 1], F32)
        clip = sb("clip3", [1, 1], F32)
        sig = sb("sig3", [1, 1], F32)
        sh_ = sb("sh3", [1, 1], F32)
        rh = sb("rh3", [1, 1], F32)
        rhbc = sb("rhbc", [P, 1], F32)
        shbc = sb("shbc", [P, 1], F32)
        dps = [ps(f"dps{i}", [P, W], F32) for i in range(2 * NB3)]
        pM = ps("pM3", [P, W], F32)
        pr = Prog()
        E = pr.emit
        bmdT = [Buf() for _ in range(IC)]
        bgin = [Buf() for _ in range(NG)]
        bn2i = [Buf() for _ in range(NB3)]
        bn2b = [Buf() for _ in range(NB3)]
        bxrt = [Buf() for _ in range(NX)]
        btmp = [Buf() for _ in range(NB3)]
        by2 = [Buf() for _ in range(NB3)]
        boutt = [Buf() for _ in range(NB3)]
        bstats = [Buf() for _ in range(NB3)]
        bmvA = [Buf() for _ in range(NB3)]
        bmvB = [Buf() for _ in range(NB3)]
        bmv = [Buf() for _ in range(NB3)]
        bdmq = [Buf() for _ in range(NB3)]
        bdm2 = [Buf() for _ in range(NB3)]
        bvs = [Buf() for _ in range(NB3)]
        bsdv = [Buf() for _ in range(NB3)]
        brstd = [Buf() for _ in range(NB3)]
        bnb = [Buf() for _ in range(NB3)]
        beye = Buf(); bones = Buf()
        bepsc = Buf()
        bamrow = Buf(); bamg = Buf(); bclip = Buf(); bsig = Buf()
        bsh = Buf(); brh = Buf(); brhbc = Buf(); bshbc = Buf()
        bdps = [Buf() for _ in range(2 * NB3)]
        bpM = Buf()
        bout_d = [Buf() for _ in range(NT)]

        E("sync", lambda h: h.dma_start(out=eye[:], in_=eye_h.ap()), writes=[beye], dma=True)
        E("gpsimd", lambda h: h.memset(ones[:], 1.0), writes=[bones])
        E("gpsimd", lambda h: h.memset(epsc[:], EPS), writes=[bepsc])
        E("sync", lambda h: h.dma_start(out=amrow[:], in_=am_d), writes=[bamrow], dma=True)
        E("vector", lambda h: h.reduce_max(out=amg[:], in_=amrow[:], axis=AX.X), reads=[bamrow], writes=[bamg])
        E("sync", lambda h: h.dma_start(out=clip[:], in_=clip_d), writes=[bclip], dma=True)
        E("scalar", lambda h: h.activation(out=sig[:], in_=clip[:], func=AF.Sigmoid), reads=[bclip], writes=[bsig])
        E("vector", lambda h: h.tensor_scalar(out=sh_[:], in0=sig[:], scalar1=amg[:], scalar2=INV127, op0=ALU.mult, op1=ALU.mult), reads=[bsig], sreads=[bamg], writes=[bsh])
        E("vector", lambda h: h.reciprocal(out=rh[:], in_=sh_[:]), reads=[bsh], writes=[brh])
        E("tensor", lambda h: h.matmul(out=pM[:, 0:1], lhsT=ones[0:1, :], rhs=rh[:], start=True, stop=True), reads=[bones, brh], writes=[bpM])
        E("vector", lambda h: h.tensor_copy(out=rhbc[:], in_=pM[:, 0:1]), reads=[bpM], writes=[brhbc])
        E("tensor", lambda h: h.matmul(out=pM[:, 1:2], lhsT=ones[0:1, :], rhs=sh_[:], start=True, stop=True), reads=[bones, bsh], writes=[bpM])
        E("vector", lambda h: h.tensor_copy(out=shbc[:], in_=pM[:, 1:2]), reads=[bpM], writes=[bshbc])
        for icc in range(IC):
            def ldm(h, icc=icc):
                return h.dma_start(out=mdT[:, icc * D:(icc + 1) * D], in_=mdt_d[icc])
            E("sync", ldm, writes=[bmdT[icc]], dma=True, lane=bmdT[icc % 2])

        def stage_A(tt):
            j = tt % NB3
            jg = tt % NG
            jx = tt % NX
            E("sync", lambda h, tt=tt, jg=jg: h.dma_start(out=gin[jg][:], in_=g_d[tt].rearrange("p ic f -> p (ic f)")), writes=[bgin[jg]], dma=True)
            E("scalar", lambda h, j=j, jg=jg: h.activation(out=n2i[j][:], in_=gin[jg][:], func=AF.Copy, scale=rhbc[:]), reads=[bgin[jg], brhbc], writes=[bn2i[j]])
            E("scalar", lambda h, j=j: h.copy(out=n2b[j][:], in_=n2i[j][:]), reads=[bn2i[j]], writes=[bn2b[j]])
            E("sync", lambda h, tt=tt, jx=jx: h.dma_start(out=xrt[jx][:], in_=xres_d[tt * P:(tt + 1) * P, :]), writes=[bxrt[jx]], dma=True)

        def stage_B(tt):
            j = tt % NB3
            for n in range(2):
                pp = 2 * (tt % 3) + n
                for icc in range(IC):
                    def mm(h, j=j, n=n, icc=icc, pp=pp):
                        return h.matmul(out=dps[pp][:], lhsT=n2b[j][:, icc * P:(icc + 1) * P],
                                        rhs=mdT[:, icc * D + n * W: icc * D + n * W + W],
                                        start=(icc == 0), stop=(icc == IC - 1))
                    pr.emit("tensor", mm, reads=[bn2b[j], bmdT[icc]], writes=[bdps[pp]])

        def stage_C(tt):
            j = tt % NB3
            jx = tt % NX
            for n in range(2):
                pp = 2 * (tt % 3) + n
                E("vector", lambda h, j=j, jx=jx, n=n, pp=pp: h.scalar_tensor_tensor(out=y2[j][:, n * W:(n + 1) * W], in0=dps[pp][:], scalar=shbc[:], in1=xrt[jx][:, n * W:(n + 1) * W], op0=ALU.mult, op1=ALU.add), reads=[bdps[pp], bxrt[jx]], sreads=[bshbc], writes=[by2[j]])
            for n in range(2):
                E("vector", lambda h, j=j, n=n: h.bn_stats(out=stats[j][:, n * 6:(n + 1) * 6], in_=y2[j][:, n * W:(n + 1) * W]), reads=[by2[j]], writes=[bstats[j]])
            E("vector", lambda h, j=j: h.bn_aggr(out=mvA[j][:], in_=stats[j][:, 0:6]), reads=[bstats[j]], writes=[bmvA[j]])
            E("vector", lambda h, j=j: h.bn_aggr(out=mvB[j][:], in_=stats[j][:, 6:12]), reads=[bstats[j]], writes=[bmvB[j]])
            E("gpsimd", lambda h, j=j: h.tensor_scalar(out=mv[j][:, 0:1], in0=mvA[j][:, 0:1], scalar1=mvB[j][:, 0:1], scalar2=0.5, op0=ALU.add, op1=ALU.mult), reads=[bmvA[j]], sreads=[bmvB[j]], writes=[bmv[j]])
            E("gpsimd", lambda h, j=j: h.tensor_scalar(out=dmq[j][:], in0=mvA[j][:, 0:1], scalar1=mvB[j][:, 0:1], scalar2=None, op0=ALU.subtract), reads=[bmvA[j]], sreads=[bmvB[j]], writes=[bdmq[j]])
            E("gpsimd", lambda h, j=j: h.tensor_scalar(out=dm2[j][:], in0=dmq[j][:], scalar1=dmq[j][:], scalar2=0.25, op0=ALU.mult, op1=ALU.mult), reads=[bdmq[j]], sreads=[bdmq[j]], writes=[bdm2[j]])
            E("gpsimd", lambda h, j=j: h.tensor_scalar(out=vs[j][:], in0=mvA[j][:, 1:2], scalar1=mvB[j][:, 1:2], scalar2=0.5, op0=ALU.add, op1=ALU.mult), reads=[bmvA[j]], sreads=[bmvB[j]], writes=[bvs[j]])
            E("gpsimd", lambda h, j=j: h.tensor_add(out=mv[j][:, 1:2], in0=vs[j][:], in1=dm2[j][:]), reads=[bdm2[j], bvs[j]], writes=[bmv[j]])
            E("scalar", lambda h, j=j: h.activation(out=sdv[j][:], in_=mv[j][:, 1:2], func=AF.Sqrt, bias=epsc[:]), reads=[bmv[j], bepsc], writes=[bsdv[j]])
            E("vector", lambda h, j=j: h.reciprocal(out=rstd[j][:], in_=sdv[j][:]), reads=[bsdv[j]], writes=[brstd[j]])
            E("gpsimd", lambda h, j=j: h.tensor_scalar(out=nb[j][:], in0=mv[j][:, 0:1], scalar1=rstd[j][:], scalar2=-1.0, op0=ALU.mult, op1=ALU.mult), reads=[bmv[j]], sreads=[brstd[j]], writes=[bnb[j]])
            E("scalar", lambda h, j=j: h.activation(out=outt[j][:], in_=y2[j][:], func=AF.Identity, scale=rstd[j][:], bias=nb[j][:]), reads=[by2[j], brstd[j], bnb[j]], writes=[boutt[j]])
            E("scalar", lambda h, tt=tt, j=j: h.dma_start(out=out_d[tt * P:(tt + 1) * P, :], in_=outt[j][:]), reads=[boutt[j]], writes=[bout_d[tt]], dma=True, lane=boutt[j])

        LAG = 2
        for t0 in range(LAG):
            stage_A(t0)
        for tt in range(NT):
            if tt + LAG < NT:
                stage_A(tt + LAG)
            stage_B(tt)
            stage_C(tt)

        block = ctx.enter_context(nc.Block())
        pr.replay(nc, block, ctx)
    return nc


_CACHE = {}


def _get(name, builder):
    if name not in _CACHE:
        _CACHE[name] = builder()
    return _CACHE[name]


def kernel(L0_attn_x, L0_attn_filt, L0_ffn_clip_x, L0_ffn_clip_down,
           L0_ffn_w_gate, L0_ffn_w_up, L0_ffn_w_down):
    x = np.ascontiguousarray(np.asarray(L0_attn_x, dtype=np.float32))
    filt = np.asarray(L0_attn_filt, dtype=np.float32)
    clip_x = np.asarray(L0_ffn_clip_x, dtype=np.float32).reshape(1, 1)
    clip_down = np.asarray(L0_ffn_clip_down, dtype=np.float32).reshape(1, 1)
    wg = np.asarray(L0_ffn_w_gate, dtype=np.float32)
    wu = np.asarray(L0_ffn_w_up, dtype=np.float32)
    wd = np.asarray(L0_ffn_w_down, dtype=np.float32)

    gpad = np.concatenate([np.zeros(P, np.float32), filt[::-1], np.zeros(P, np.float32)])
    sw = np.lib.stride_tricks.sliding_window_view(gpad, P)
    cc, uu = np.meshgrid(np.arange(FC), np.arange(P), indexing="ij")
    F_host = np.ascontiguousarray(sw[2175 - 128 * cc - uu], dtype=np.float32)

    wg_nat = np.ascontiguousarray(wg.reshape(IC, P, D))
    wu_nat = np.ascontiguousarray(wu.reshape(IC, P, D))
    wd_nat = np.ascontiguousarray(wd.reshape(HC, P, I))
    wd_t = np.ascontiguousarray(wd.T.reshape(IC, P, D))

    nc1 = _get("l1", build_l1)
    in1 = [dict(x_in=np.ascontiguousarray(x[c]), f_in=F_host) for c in range(B)]
    r1 = run_bass_kernel_spmd(nc1, in1, core_ids=list(range(B)))
    amax1_all = np.concatenate([np.asarray(r1.results[c]["amax1"]) for c in range(B)], axis=1)
    xres = [np.asarray(r1.results[c]["xres"]) for c in range(B)]

    nc2 = _get("l2", build_l2)
    in2 = [dict(xres_in=xres[c], amax1_all=amax1_all, clip_x_in=clip_x,
                wg_nat=wg_nat, wu_nat=wu_nat,
                wd_nat=wd_nat, wd_t=wd_t) for c in range(B)]
    r2 = run_bass_kernel_spmd(nc2, in2, core_ids=list(range(B)))
    amax2_all = np.concatenate([np.asarray(r2.results[c]["amax2"]) for c in range(B)], axis=1)

    nc3 = _get("l3", build_l3)
    in3 = [dict(g_in=np.asarray(r2.results[c]["g_out"]),
                xres3_in=xres[c],
                mdt_in=np.asarray(r2.results[c]["mdt_out"]),
                amax2_all=amax2_all, clip_down_in=clip_down) for c in range(B)]
    r3 = run_bass_kernel_spmd(nc3, in3, core_ids=list(range(B)))
    out = np.stack([np.asarray(r3.results[c]["out"]) for c in range(B)], axis=0)
    return out.astype(np.float32, copy=False)


# version 33
# speedup vs baseline: 1.0172x; 1.0172x over previous
import numpy as np
from contextlib import ExitStack

import concourse.bass as bass
import concourse.mybir as mybir
from concourse.bass_utils import run_bass_kernel_spmd

F32 = mybir.dt.float32
BF16 = mybir.dt.bfloat16
I8 = mybir.dt.int8
AF = mybir.ActivationFunctionType
ALU = mybir.AluOpType
AX = mybir.AxisListType

P = 128
B = 8
S = 4096
D = 1024
HC = D // P
NT = S // P
I = 2816
IC = I // P
L = 2048
FC = 17
W = 512
NW = S // W
INV127 = float(np.float32(1.0) / np.float32(127.0))
EPS = 1e-5

ENGS = ("sync", "vector", "scalar", "tensor", "gpsimd")


class Buf:
    __slots__ = ("last_w", "readers")

    def __init__(self):
        self.last_w = None
        self.readers = []


class Prog:

    def __init__(self):
        self.ops = {e: [] for e in ENGS}
        self.cnt = {e: 0 for e in ENGS}
        self.lane_cnt = {}
        self.lane_ids = {}

    def _lane(self, obj, sw):
        key = (id(obj), sw)
        lid = self.lane_ids.get(key)
        if lid is None:
            lid = f"lane{len(self.lane_ids)}{'S' if sw else 'H'}"
            self.lane_ids[key] = lid
            self.lane_cnt[lid] = 0
        return lid

    def emit(self, eng, fn, reads=(), writes=(), sreads=(), dma=False, lane=None):
        deps = {}

        def add(ev, force=False):
            if ev is not None:
                key, c, was_dma = ev
                if (key == eng == "tensor" and not was_dma and not dma
                        and not force):
                    return
                if deps.get(key, 0) < c:
                    deps[key] = c

        for b in reads:
            add(b.last_w)
        for b in sreads:
            add(b.last_w, force=True)
        for b in writes:
            add(b.last_w)
            for ev in b.readers:
                add(ev)
        if dma:
            lobj = lane if lane is not None else (writes[0] if writes else reads[0])
            lid = self._lane(lobj, eng == "gpsimd")
            self.lane_cnt[lid] += 16
            ev_self = (lid, self.lane_cnt[lid], True)
        else:
            self.cnt[eng] += 1
            ev_self = (eng, self.cnt[eng], False)
        for b in reads:
            b.readers.append(ev_self)
        for b in sreads:
            b.readers.append(ev_self)
        for b in writes:
            b.last_w = ev_self
            b.readers = []
        self.ops[eng].append((deps, fn, ev_self))

    CAP = 240

    def _epochs(self, lid):
        return (self.lane_cnt[lid] - 16) // Prog.CAP + 1

    def replay(self, nc, block, ctx):
        sems = {e: ctx.enter_context(nc.semaphore(f"sem_{e}")) for e in ENGS}
        final = dict(self.cnt)
        for lid, tot in self.lane_cnt.items():
            for k in range(self._epochs(lid)):
                kk = f"{lid}_e{k}"
                sems[kk] = ctx.enter_context(nc.semaphore(kk))
                final[kk] = min(tot - k * Prog.CAP, Prog.CAP)

        def make_body(eng):
            def body(h):
                waited = {}

                def wait(kk, c):
                    if waited.get(kk, 0) < c:
                        h.wait_ge(sems[kk], c)
                        waited[kk] = c

                def lane_wait(key, c):
                    e = (c - 16) // Prog.CAP
                    for k in range(e):
                        wait(f"{key}_e{k}", Prog.CAP)
                    wait(f"{key}_e{e}", c - e * Prog.CAP)

                for deps, fn, ev_self in self.ops[eng]:
                    for key, c in sorted(deps.items()):
                        if key in self.lane_cnt:
                            lane_wait(key, c)
                        else:
                            wait(key, c)
                    inst = fn(h)
                    key, c, was_dma = ev_self
                    if was_dma:
                        e = (c - 16) // Prog.CAP
                        inst.then_inc(sems[f"{key}_e{e}"], 16)
                    else:
                        inst.then_inc(sems[key], 1)
                for key, c in sorted(final.items()):
                    if c and waited.get(key, 0) < c and key != eng:
                        h.wait_ge(sems[key], c)
            return body

        block.sync(make_body("sync"))
        block.vector(make_body("vector"))
        block.scalar(make_body("scalar"))
        block.tensor(make_body("tensor"))
        block.gpsimd(make_body("gpsimd"))



def build_l1(debug=False):
    nc = bass.Bass("TRN2", target_bir_lowering=False)
    x_d = nc.dram_tensor("x_in", [S, D], F32, kind="ExternalInput").ap()
    f_d = nc.dram_tensor("f_in", [FC, P, P], F32, kind="ExternalInput").ap()
    xres_d = nc.dram_tensor("xres", [S, D], F32, kind="ExternalOutput").ap()
    amax_d = nc.dram_tensor("amax1", [1, 1], F32, kind="ExternalOutput").ap()
    if debug:
        ydbg_d = nc.dram_tensor("ydbg", [2 * P, D], F32, kind="ExternalOutput").ap()
        sdbg_d = nc.dram_tensor("sdbg", [2 * P, 20], F32, kind="ExternalOutput").ap()
    dbg_sb = None
    bounce = nc.dram_tensor("bounce1", [1, P], F32)
    eye_h = nc.inline_tensor(np.eye(P, dtype=np.float32), "eye_f32_h")
    RING = 32

    with ExitStack() as ctx:
        sb = lambda n, s, d: ctx.enter_context(nc.sbuf_tensor(n, s, d))
        ps = lambda n, s, d: ctx.enter_context(nc.psum_tensor(n, s, d))
        NBUF = 4
        xf = [sb(f"xf{i}", [P, D], F32) for i in range(2)]
        ring = sb("ring", [P, RING * D], BF16)
        Ff = sb("Ff", [P, FC * P], F32)
        Fb = sb("Fb", [P, FC * P], BF16)
        eye = sb("eye", [P, P], F32)
        xrt = [sb(f"xrt{i}", [P, D], F32) for i in range(NBUF)]
        sqscr = [sb(f"sqscr{i}", [P, W], F32) for i in range(2)]
        stats = [sb(f"stats{i}", [P, 12], F32) for i in range(NBUF)]
        mvA = [sb(f"mvA{i}", [P, 2], F32) for i in range(NBUF)]
        mvB = [sb(f"mvB{i}", [P, 2], F32) for i in range(NBUF)]
        mv = [sb(f"mv{i}", [P, 2], F32) for i in range(NBUF)]
        dmq = [sb(f"dmq{i}", [P, 1], F32) for i in range(NBUF)]
        dm2 = [sb(f"dm2_{i}", [P, 1], F32) for i in range(NBUF)]
        vs = [sb(f"vs{i}", [P, 1], F32) for i in range(NBUF)]
        sdv = [sb(f"sdv{i}", [P, 1], F32) for i in range(NBUF)]
        rstd = [sb(f"rstd{i}", [P, 1], F32) for i in range(NBUF)]
        nb = [sb(f"nb{i}", [P, 1], F32) for i in range(NBUF)]
        slots = sb("slots", [P, NT], F32)
        epsc = sb("epsc", [P, 1], F32)
        acol = sb("acol", [P, 1], F32)
        arow = sb("arow", [1, P], F32)
        aout = sb("aout", [1, 1], F32)
        y = [ps(f"y{i}", [P, W], F32) for i in range(2 * NBUF)]
        if debug:
            dbg_sb = sb("dbg_sb", [P, D], F32)
        bdbg = Buf()
        pr = Prog()
        bF = Buf(); bFb = Buf(); beye = Buf()
        bxf = [Buf() for _ in range(2)]
        bring = [Buf() for _ in range(RING)]
        bxrt = [Buf() for _ in range(NBUF)]
        bsqscr = [Buf() for _ in range(2)]
        bstats = [Buf() for _ in range(NBUF)]
        bmvA = [Buf() for _ in range(NBUF)]
        bmvB = [Buf() for _ in range(NBUF)]
        bmv = [Buf() for _ in range(NBUF)]
        bdmq = [Buf() for _ in range(NBUF)]
        bdm2 = [Buf() for _ in range(NBUF)]
        bvs = [Buf() for _ in range(NBUF)]
        bsdv = [Buf() for _ in range(NBUF)]
        brstd = [Buf() for _ in range(NBUF)]
        bnb = [Buf() for _ in range(NBUF)]
        by = [Buf() for _ in range(2 * NBUF)]
        bslot = [Buf() for _ in range(NT)]
        bacol = Buf(); bbounce = Buf(); barow = Buf(); baout = Buf()
        bepsc = Buf()
        bxres_d = [Buf() for _ in range(NT)]
        bamax_d = Buf()

        pr.emit("gpsimd", lambda h: h.memset(epsc[:], EPS), writes=[bepsc])
        pr.emit("sync", lambda h: h.dma_start(out=Ff[:].rearrange("p (c i) -> p c i", c=FC), in_=f_d.transpose([1, 0, 2])), writes=[bF], dma=True)
        pr.emit("sync", lambda h: h.dma_start(out=eye[:], in_=eye_h.ap()), writes=[beye], dma=True)
        pr.emit("vector", lambda h: h.tensor_add(out=Ff[:, 0:P], in0=Ff[:, 0:P], in1=eye[:]), reads=[beye], writes=[bF])
        pr.emit("vector", lambda h: h.tensor_copy(out=Fb[:], in_=Ff[:]), reads=[bF], writes=[bFb])

        xfq = [sb(f"xfq{i}", [P, D], F32) for i in range(2)]
        bxfq = [Buf() for _ in range(2)]

        def emit_load(k):
            q = "sync" if k % 2 == 0 else "scalar"
            bx = bxf[k % 2] if k % 2 == 0 else bxfq[k % 4 // 2]
            xt = xf[k % 2] if k % 2 == 0 else xfq[k % 4 // 2]
            def ld(h, k=k, xt=xt):
                return h.dma_start(out=xt[:], in_=x_d[k * P:(k + 1) * P, :])
            pr.emit(q, ld, reads=[], writes=[bx], dma=True)
            sl = k
            def cast(h, xt=xt, sl=sl):
                return h.copy(out=ring[:, sl * D:(sl + 1) * D], in_=xt[:])
            pr.emit("scalar", cast, reads=[bx], writes=[bring[sl]])

        order = []
        lo, hi = 0, NT - 1
        while lo <= hi:
            order.append(lo)
            if hi != lo:
                order.append(hi)
            lo += 1
            hi -= 1

        load_order = []
        seen = set()
        for T in order:
            cmax = min(FC - 1, NT - 1 - T)
            for c in range(cmax + 1):
                k = T + c
                if k not in seen:
                    seen.add(k)
                    load_order.append(k)
        for k in load_order[:17]:
            emit_load(k)
        rest = load_order[17:]

        def stage_C2(T, par):
            def amx(h, par=par, T=T):
                return h.reduce_max(out=slots[:, T:T + 1], in_=xrt[par][:], axis=AX.X, apply_absolute_value=True)
            pr.emit("vector", amx, reads=[bxrt[par]], writes=[bslot[T]])
            def st(h, par=par, T=T):
                return h.dma_start(out=xres_d[T * P:(T + 1) * P, :], in_=xrt[par][:])
            pr.emit("scalar", st, reads=[bxrt[par]], writes=[bxres_d[T]], dma=True, lane=bxrt[par])

        prev = None
        for ti, T in enumerate(order):
            par = ti % NBUF
            if ti < len(rest):
                emit_load(rest[ti])
            cmax = min(FC - 1, NT - 1 - T)
            for hh in range(2):
                yb = y[par * 2 + hh]
                byb = by[par * 2 + hh]
                for c in range(cmax + 1):
                    sl = (T + c) % RING
                    def mm(h, yb=yb, c=c, sl=sl, hh=hh, cmax=cmax):
                        return h.matmul(out=yb[:], lhsT=Fb[:, c * P:(c + 1) * P],
                                        rhs=ring[:, sl * D + hh * W: sl * D + hh * W + W],
                                        start=(c == 0), stop=(c == cmax))
                    pr.emit("tensor", mm, reads=[bFb, bring[sl]], writes=[byb])
            for hh in range(2):
                def rs(h, par=par, hh=hh):
                    return h.reduce_sum(out=stats[par][:, hh:hh + 1], in_=y[par * 2 + hh][:], axis=AX.X)
                pr.emit("vector", rs, reads=[by[par * 2 + hh]], writes=[bstats[par]])
                def sqa(h, par=par, hh=hh):
                    return h.activation(out=sqscr[par % 2][:], in_=y[par * 2 + hh][:], func=AF.Square, accum_out=stats[par][:, 2 + hh:3 + hh])
                pr.emit("scalar", sqa, reads=[by[par * 2 + hh]], writes=[bstats[par], bsqscr[par % 2]])
            pr.emit("vector", lambda h, par=par: h.tensor_scalar(out=mv[par][:, 0:1], in0=stats[par][:, 0:1], scalar1=stats[par][:, 1:2], scalar2=1.0 / D, op0=ALU.add, op1=ALU.mult), reads=[bstats[par]], sreads=[bstats[par]], writes=[bmv[par]])
            pr.emit("vector", lambda h, par=par: h.tensor_scalar(out=dmq[par][:], in0=stats[par][:, 2:3], scalar1=stats[par][:, 3:4], scalar2=1.0 / D, op0=ALU.add, op1=ALU.mult), reads=[bstats[par]], sreads=[bstats[par]], writes=[bdmq[par]])
            pr.emit("vector", lambda h, par=par: h.tensor_scalar(out=dm2[par][:], in0=mv[par][:, 0:1], scalar1=mv[par][:, 0:1], scalar2=None, op0=ALU.mult), reads=[bmv[par]], sreads=[bmv[par]], writes=[bdm2[par]])
            pr.emit("vector", lambda h, par=par: h.tensor_sub(out=mv[par][:, 1:2], in0=dmq[par][:], in1=dm2[par][:]), reads=[bdmq[par], bdm2[par]], writes=[bmv[par]])
            def sq(h, par=par):
                return h.activation(out=sdv[par][:], in_=mv[par][:, 1:2], func=AF.Sqrt, bias=epsc[:])
            pr.emit("scalar", sq, reads=[bmv[par], bepsc], writes=[bsdv[par]])
            def rc(h, par=par):
                return h.reciprocal(out=rstd[par][:], in_=sdv[par][:])
            pr.emit("vector", rc, reads=[bsdv[par]], writes=[brstd[par]])
            pr.emit("gpsimd", lambda h, par=par: h.tensor_scalar(out=nb[par][:], in0=mv[par][:, 0:1], scalar1=rstd[par][:], scalar2=-1.0, op0=ALU.mult, op1=ALU.mult), reads=[bmv[par]], sreads=[brstd[par]], writes=[bnb[par]])
            for hh in range(2):
                def norm(h, par=par, hh=hh):
                    return h.activation(out=xrt[par][:, hh * W:(hh + 1) * W], in_=y[par * 2 + hh][:],
                                        func=AF.Identity, scale=rstd[par][:], bias=nb[par][:])
                pr.emit("scalar", norm, reads=[by[par * 2 + hh], brstd[par], bnb[par]], writes=[bxrt[par]])
            if prev is not None:
                stage_C2(*prev)
            prev = (T, par)
        stage_C2(*prev)

        pr.emit("vector", lambda h: h.reduce_max(out=acol[:], in_=slots[:], axis=AX.X), reads=bslot, writes=[bacol])
        pr.emit("scalar", lambda h: h.dma_start(out=bounce.ap().transpose([1, 0]), in_=acol[:]), reads=[bacol], writes=[bbounce], dma=True)
        pr.emit("sync", lambda h: h.dma_start(out=arow[:], in_=bounce.ap()), reads=[bbounce], writes=[barow], dma=True)
        pr.emit("vector", lambda h: h.reduce_max(out=aout[:], in_=arow[:], axis=AX.X), reads=[barow], writes=[baout])
        pr.emit("scalar", lambda h: h.dma_start(out=amax_d, in_=aout[:]), reads=[baout], writes=[bamax_d], dma=True)

        block = ctx.enter_context(nc.Block())
        pr.replay(nc, block, ctx)
    return nc



def build_l2(debug=False):
    nc = bass.Bass("TRN2", target_bir_lowering=False)
    xres_d = nc.dram_tensor("xres_in", [S, D], F32, kind="ExternalInput").ap()
    am_d = nc.dram_tensor("amax1_all", [1, B], F32, kind="ExternalInput").ap()
    clip_d = nc.dram_tensor("clip_x_in", [1, 1], F32, kind="ExternalInput").ap()
    wgn_d = nc.dram_tensor("wg_nat", [IC, P, D], F32, kind="ExternalInput").ap()
    wun_d = nc.dram_tensor("wu_nat", [IC, P, D], F32, kind="ExternalInput").ap()
    wdn_d = nc.dram_tensor("wd_nat", [HC, P, I], F32, kind="ExternalInput").ap()
    wdt_d = nc.dram_tensor("wd_t", [IC, P, D], F32, kind="ExternalInput").ap()

    g_d = nc.dram_tensor("g_out", [NT, P, IC, P], F32, kind="ExternalOutput").ap()
    mdt_d = nc.dram_tensor("mdt_out", [IC, P, D], BF16, kind="ExternalOutput").ap()
    sd_d = nc.dram_tensor("sd_out", [P, HC], F32, kind="ExternalOutput").ap()
    amax2_d = nc.dram_tensor("amax2", [1, 1], F32, kind="ExternalOutput").ap()
    bounce = nc.dram_tensor("bounce2", [1, P], F32)
    brow = nc.dram_tensor("brow", [1, P], F32)
    brow2 = nc.dram_tensor("brow2", [1, P], F32)
    eye_h = nc.inline_tensor(np.eye(P, dtype=np.float32), "eye2_f32_h")
    if debug:
        scdbg_d = nc.dram_tensor("scdbg", [P, 8 + 2 * IC], F32, kind="ExternalOutput").ap()
        xqdbg_d = nc.dram_tensor("xqdbg", [P, D], F32, kind="ExternalOutput").ap()

    with ExitStack() as ctx:
        sb = lambda n, s, d: ctx.enter_context(nc.sbuf_tensor(n, s, d))
        ps = lambda n, s, d: ctx.enter_context(nc.psum_tensor(n, s, d))
        xq = sb("xq", [P, HC * S], BF16)
        mgT = sb("mgT", [P, IC * D], BF16)
        muT = sb("muT", [P, IC * D], BF16)
        wnat = [sb(f"wnat{i}", [P, D], F32) for i in range(2)]
        wT = [sb(f"wT{i}", [P, D], F32) for i in range(2)]
        wcod = [sb(f"wcod{i}", [P, D], I8) for i in range(2)]
        cnat = [sb(f"cnat{i}", [P, D], BF16) for i in range(2)]
        ctm2 = [sb(f"ctm2_{i}", [P, D], BF16) for i in range(2)]
        Rd = sb("Rd", [P, D], F32)

        xrt = [sb(f"xrt{i}", [P, D], F32) for i in range(2)]
        xcod = [sb(f"xcod{i}", [P, D], I8) for i in range(2)]
        ctm = [sb(f"ctm{i}", [P, D], BF16) for i in range(2)]
        tsilu = [sb(f"tsilu{i}", [P, W], F32) for i in range(2)]
        gsb = [sb(f"gsb{i}", [P, W], F32) for i in range(2)]
        a_tmp = [sb(f"atmp{i}", [P, 1], F32) for i in range(2)]
        rcol = [sb(f"rcol{i}", [P, 1], F32) for i in range(2)]
        rcolg = sb("rcolg", [P, 1], F32)
        rcolu = sb("rcolu", [P, 1], F32)
        s_g = sb("s_g", [P, IC], F32)
        s_u = sb("s_u", [P, IC], F32)
        sgc = sb("sgc", [P, IC], F32)
        suc = sb("suc", [P, IC], F32)
        s_d = sb("s_d", [P, HC], F32)
        rd_acc = [sb(f"rdacc{i}", [P, 1], F32) for i in range(2)]
        slots2 = sb("slots2", [P, IC * NW], F32)
        eye = sb("eye2", [P, P], F32)
        eyeb = sb("eyeb", [P, P], BF16)
        ones = sb("ones", [P, P], F32)
        amrow = sb("amrow", [1, B], F32)
        amg = sb("amg", [1, 1], F32)
        clip = sb("clip", [1, 1], F32)
        sig = sb("sig", [1, 1], F32)
        sx = sb("sx", [1, 1], F32)
        rx = sb("rx", [1, 1], F32)
        rxbc = sb("rxbc", [P, 1], F32)
        sxbc = sb("sxbc", [P, 1], F32)
        acol = sb("acol2", [P, 1], F32)
        arow = sb("arow2", [1, P], F32)
        aout = sb("aout2", [1, 1], F32)
        gps = [ps(f"gps{i}", [P, W], F32) for i in range(3)]
        ups = [ps(f"ups{i}", [P, W], F32) for i in range(3)]
        pT = ps("pT", [P, D], BF16)
        pT2 = ps("pT2", [P, D], BF16)
        pr = Prog()
        bxq = [Buf() for _ in range(NT)]
        bmgT = [Buf() for _ in range(IC)]
        bmuT = [Buf() for _ in range(IC)]
        bwnat = [Buf() for _ in range(2)]
        bwT = [Buf() for _ in range(2)]
        bwcod = [Buf() for _ in range(2)]
        bcnat = [Buf() for _ in range(2)]
        bctm2 = [Buf() for _ in range(2)]
        bRd = Buf()
        bxrt = [Buf() for _ in range(2)]
        bxcod = [Buf() for _ in range(2)]
        bctm = [Buf() for _ in range(2)]
        btsilu = [Buf() for _ in range(2)]
        bgsb = [Buf() for _ in range(2)]
        ba_tmp = [Buf() for _ in range(2)]
        brcol = [Buf() for _ in range(2)]
        brcolg = Buf()
        brcolu = Buf()
        bs_g = [Buf() for _ in range(IC)]
        bs_u = [Buf() for _ in range(IC)]
        bsgc = [Buf() for _ in range(IC)]
        bsuc = [Buf() for _ in range(IC)]
        bs_d = [Buf() for _ in range(HC)]
        brd_acc = [Buf() for _ in range(2)]
        bslot2 = [Buf() for _ in range(IC * NW)]
        beye = Buf(); beyeb = Buf(); bones = Buf()
        bamrow = Buf(); bamg = Buf(); bclip = Buf(); bsig = Buf()
        bbrow = Buf(); bbrow2 = Buf()
        bsx = Buf(); brx = Buf(); brxbc = Buf(); bsxbc = Buf()
        bgps = [Buf() for _ in range(3)]
        bups = [Buf() for _ in range(3)]
        bpT = [Buf() for _ in range(HC)]
        _pT2bufs = [Buf() for _ in range(HC)]
        bpT2g = [_pT2bufs, _pT2bufs]
        bpM = Buf()
        bacol = Buf(); bbounce = Buf(); barow = Buf(); baout = Buf()
        bg_d = [Buf() for _ in range(NT)]
        bmdt_d = [Buf() for _ in range(IC)]
        bsd_d = Buf(); bamax2_d = Buf()

        E = pr.emit

        E("sync", lambda h: h.dma_start(out=eye[:], in_=eye_h.ap()), writes=[beye], dma=True)
        E("vector", lambda h: h.tensor_copy(out=eyeb[:], in_=eye[:]), reads=[beye], writes=[beyeb])
        E("gpsimd", lambda h: h.memset(ones[:], 1.0), writes=[bones])

        E("sync", lambda h: h.dma_start(out=amrow[:], in_=am_d), writes=[bamrow], dma=True)
        E("vector", lambda h: h.reduce_max(out=amg[:], in_=amrow[:], axis=AX.X), reads=[bamrow], writes=[bamg])
        E("sync", lambda h: h.dma_start(out=clip[:], in_=clip_d), writes=[bclip], dma=True)
        E("scalar", lambda h: h.activation(out=sig[:], in_=clip[:], func=AF.Sigmoid), reads=[bclip], writes=[bsig])
        E("vector", lambda h: h.tensor_scalar(out=sx[:], in0=sig[:], scalar1=amg[:], scalar2=INV127, op0=ALU.mult, op1=ALU.mult), reads=[bsig], sreads=[bamg], writes=[bsx])
        E("vector", lambda h: h.reciprocal(out=rx[:], in_=sx[:]), reads=[bsx], writes=[brx])
        E("tensor", lambda h: h.matmul(out=gps[0][:, 0:1], lhsT=ones[0:1, :], rhs=rx[:], start=True, stop=True), reads=[bones, brx], writes=[bgps[0]])
        E("vector", lambda h: h.tensor_copy(out=rxbc[:], in_=gps[0][:, 0:1]), reads=[bgps[0]], writes=[brxbc])
        E("tensor", lambda h: h.matmul(out=gps[0][:, 1:2], lhsT=ones[0:1, :], rhs=sx[:], start=True, stop=True), reads=[bones, bsx], writes=[bgps[0]])
        E("vector", lambda h: h.tensor_copy(out=sxbc[:], in_=gps[0][:, 1:2]), reads=[bgps[0]], writes=[bsxbc])

        if debug:
            scdbg = sb("scdbg_sb", [P, 8 + 2 * IC], F32)
            bscdbg = Buf()

        def emit_xq_tile(t):
            j = t % 2
            E("sync", lambda h, t=t, j=j: h.dma_start(out=xrt[j][:], in_=xres_d[t * P:(t + 1) * P, :]), writes=[bxrt[j]], dma=True)
            E("scalar", lambda h, j=j: h.activation(out=xcod[j][:], in_=xrt[j][:], func=AF.Copy, scale=rxbc[:]), reads=[bxrt[j], brxbc], writes=[bxcod[j]])
            E("scalar", lambda h, j=j: h.copy(out=ctm[j][:], in_=xcod[j][:]), reads=[bxcod[j]], writes=[bctm[j]])
            for hc in range(HC):
                E("tensor", lambda h, j=j, hc=hc: h.transpose(out=pT[:, hc * P:(hc + 1) * P], in_=ctm[j][:, hc * P:(hc + 1) * P], identity=eyeb[:]), reads=[bctm[j], beyeb], writes=[bpT[hc]])
            def evac(h, t=t):
                dst = xq[:].rearrange("p (hc s) -> p hc s", hc=HC)[:, :, t * P:(t + 1) * P]
                src = pT[:].rearrange("p (hc f) -> p hc f", hc=HC)
                return h.tensor_copy(out=dst, in_=src)
            E("vector", evac, reads=bpT, writes=[bxq[t]])

        def _w_sel(which):
            return ((wgn_d, mgT, bmgT, s_g, bs_g, sgc, bsgc) if which == "g"
                    else (wun_d, muT, bmuT, s_u, bs_u, suc, bsuc))

        def w_chunk_s1(which, ic):
            j = 0 if which == "g" else 1
            nat_d, dst, bdst, scol, bscol, ccol, bccol = _w_sel(which)
            E("sync", lambda h, ic=ic, j=j, nat_d=nat_d: h.dma_start(out=wnat[j][:], in_=nat_d[ic]), writes=[bwnat[j]], dma=True)
            E("vector", lambda h, j=j: h.reduce_max(out=a_tmp[j][:], in_=wnat[j][:], axis=AX.X, apply_absolute_value=True), reads=[bwnat[j]], writes=[ba_tmp[j]])
            E("vector", lambda h, j=j, scol=scol, ic=ic: h.tensor_scalar_mul(out=scol[:, ic:ic + 1], in0=a_tmp[j][:], scalar1=INV127), reads=[ba_tmp[j]], writes=[bscol[ic]])
            E("vector", lambda h, scol=scol, ic=ic, ccol=ccol: h.tensor_mul(out=ccol[:, ic:ic + 1], in0=scol[:, ic:ic + 1], in1=sxbc[:]), reads=[bscol[ic], bsxbc], writes=[bccol[ic]])
            E("vector", lambda h, j=j, wr=(brcolg if which == "g" else brcolu), rt=(rcolg if which == "g" else rcolu), scol=scol, ic=ic: h.reciprocal(out=rt[:], in_=scol[:, ic:ic + 1]), reads=[bscol[ic]], writes=[(brcolg if which == "g" else brcolu)])

        def w_chunk_s2a(which, ic):
            j = 0 if which == "g" else 1
            rt, brt = (rcolg, brcolg) if which == "g" else (rcolu, brcolu)
            cj = 0 if which == "g" else 1
            E("scalar", lambda h, j=j, rt=rt, brt=brt, cj=cj: h.activation(out=wcod[cj][:], in_=wnat[j][:], func=AF.Copy, scale=rt[:]), reads=[bwnat[j], brt], writes=[bwcod[cj]])
            E("scalar", lambda h, cj=cj: h.copy(out=cnat[cj][:], in_=wcod[cj][:]), reads=[bwcod[cj]], writes=[bcnat[cj]])

        def w_chunk_s2b(which, ic):
            cj = 0 if which == "g" else 1
            for hc in range(HC):
                E("tensor", lambda h, cj=cj, hc=hc: h.transpose(out=pT2[:, hc * P:(hc + 1) * P], in_=cnat[cj][:, hc * P:(hc + 1) * P], identity=eyeb[:]), reads=[bcnat[cj], beyeb], writes=[bpT2g[cj][hc]])

        def w_chunk_s3(which, ic):
            nat_d, dst, bdst, scol, bscol, ccol, bccol = _w_sel(which)
            cj = 0 if which == "g" else 1
            def evac(h, dst=dst, ic=ic, cj=cj):
                return h.copy(out=dst[:, ic * D:(ic + 1) * D], in_=pT2[:])
            E("scalar", evac, reads=bpT2g[cj], writes=[bdst[ic]])

        def wd_nat_piece(hc, k):
            pieces = [(0, D), (D, D), (2 * D, I - 2 * D)]
            off, ln = pieces[k]
            j = k % 2
            E("sync", lambda h, hc=hc, off=off, ln=ln: h.dma_start(out=xrt[1][:, 0:ln], in_=wdn_d[hc, :, off:off + ln]), writes=[bxrt[1]], dma=True)
            E("vector", lambda h, j=j, ln=ln: h.reduce_max(out=a_tmp[j][:], in_=xrt[1][:, 0:ln], axis=AX.X, apply_absolute_value=True), reads=[bxrt[1]], writes=[ba_tmp[j]])
            if k == 0:
                E("vector", lambda h: h.tensor_copy(out=rd_acc[0][:], in_=a_tmp[0][:]), reads=[ba_tmp[0]], writes=[brd_acc[0]])
            else:
                E("vector", lambda h, k=k, j=j: h.tensor_max(out=rd_acc[k % 2][:], in0=rd_acc[(k - 1) % 2][:], in1=a_tmp[j][:]), reads=[brd_acc[(k - 1) % 2], ba_tmp[j]], writes=[brd_acc[k % 2]])

        def wd_nat_finish(hc):
            E("vector", lambda h, hc=hc: h.tensor_scalar_mul(out=s_d[:, hc:hc + 1], in0=rd_acc[0][:], scalar1=INV127), reads=[brd_acc[0]], writes=[bs_d[hc]])
            j = hc % 2
            E("vector", lambda h, j=j, hc=hc: h.reciprocal(out=rcol[j][:], in_=s_d[:, hc:hc + 1]), reads=[bs_d[hc]], writes=[brcol[j]])
            E("scalar", lambda h, j=j: h.dma_start(out=brow.ap().transpose([1, 0]), in_=rcol[j][:]), reads=[brcol[j]], writes=[bbrow], dma=True, lane=bbrow)
            def ldRd(h, hc=hc):
                return h.dma_start(out=Rd[:, hc * P:(hc + 1) * P], in_=brow.ap().partition_broadcast(P))
            E("gpsimd", ldRd, reads=[bbrow], writes=[bRd], dma=True, lane=bRd)
            E("scalar", lambda h, hc=hc: h.dma_start(out=brow2.ap().transpose([1, 0]), in_=s_d[:, hc:hc + 1]), reads=[bs_d[hc]], writes=[bbrow2], dma=True, lane=bbrow2)
            def ldSd(h, hc=hc):
                return h.dma_start(out=xrt[0][:, hc * P:(hc + 1) * P], in_=brow2.ap().partition_broadcast(P))
            E("gpsimd", ldSd, reads=[bbrow2], writes=[bxrt[0]], dma=True, lane=bxrt[0])

        def emit_wd_t_load(icc):
            j = icc % 2
            E("sync", lambda h, icc=icc, j=j: h.dma_start(out=wT[j][:], in_=wdt_d[icc]), writes=[bwT[j]], dma=True)

        def emit_wd_t_compute(icc):
            j = icc % 2
            E("vector", lambda h, j=j: h.tensor_mul(out=wcod[j][:], in0=wT[j][:], in1=Rd[:]), reads=[bwT[j], bRd], writes=[bwcod[j]])
            E("gpsimd", lambda h, j=j: h.tensor_mul(out=ctm2[j][:], in0=wcod[j][:], in1=xrt[0][:]), reads=[bwcod[j], bxrt[0]], writes=[bctm2[j]])
            E("gpsimd", lambda h, j=j, icc=icc: h.dma_start(out=mdt_d[icc], in_=ctm2[j][:]), reads=[bctm2[j]], writes=[bmdt_d[icc]], dma=True, lane=bctm2[j])

        def emit_sweep(ic, w):
            par = (ic * NW + w) % 3
            sx2 = (ic * NW + w) % 2
            for hc in range(HC):
                def mmg(h, ic=ic, w=w, hc=hc, par=par):
                    return h.matmul(out=gps[par][:], lhsT=mgT[:, ic * D + hc * P: ic * D + (hc + 1) * P],
                                    rhs=xq[:, hc * S + w * W: hc * S + (w + 1) * W],
                                    start=(hc == 0), stop=(hc == HC - 1))
                pr.emit("tensor", mmg, reads=[bmgT[ic]] + bxq[4 * w:4 * w + 4], writes=[bgps[par]])
            for hc in range(HC):
                def mmu(h, ic=ic, w=w, hc=hc, par=par):
                    return h.matmul(out=ups[par][:], lhsT=muT[:, ic * D + hc * P: ic * D + (hc + 1) * P],
                                    rhs=xq[:, hc * S + w * W: hc * S + (w + 1) * W],
                                    start=(hc == 0), stop=(hc == HC - 1))
                pr.emit("tensor", mmu, reads=[bmuT[ic]] + bxq[4 * w:4 * w + 4], writes=[bups[par]])
            E("scalar", lambda h, par=par, sx2=sx2, ic=ic: h.activation(out=tsilu[sx2][:], in_=gps[par][:], func=AF.Silu, scale=sgc[:, ic:ic + 1]), reads=[bgps[par], bsgc[ic]], writes=[btsilu[sx2]])
            E("vector", lambda h, par=par, sx2=sx2, ic=ic: h.scalar_tensor_tensor(out=gsb[sx2][:], in0=ups[par][:], scalar=suc[:, ic:ic + 1], in1=tsilu[sx2][:], op0=ALU.mult, op1=ALU.mult), reads=[bups[par], btsilu[sx2]], sreads=[bsuc[ic]], writes=[bgsb[sx2]])
            E("vector", lambda h, sx2=sx2, ic=ic, w=w: h.reduce_max(out=slots2[:, ic * NW + w: ic * NW + w + 1], in_=gsb[sx2][:], axis=AX.X, apply_absolute_value=True), reads=[bgsb[sx2]], writes=[bslot2[ic * NW + w]])
            def stg(h, sx2=sx2, ic=ic, w=w):
                dst = g_d[4 * w:4 * w + 4, :, ic, :].transpose([1, 0, 2])
                src = gsb[sx2][:].rearrange("p (tt f) -> p tt f", tt=4)
                return h.dma_start(out=dst, in_=src)
            E("gpsimd", stg, reads=[bgsb[sx2]], writes=[bg_d[4 * w + k] for k in range(4)], dma=True, lane=bgsb[sx2])

        for t in range(8):
            emit_xq_tile(t)
        for which, ic in (("g", 0), ("u", 0), ("g", 1), ("u", 1), ("g", 2), ("u", 2)):
            w_chunk_s1(which, ic)
            w_chunk_s2a(which, ic)
            w_chunk_s2b(which, ic)
            w_chunk_s3(which, ic)
        down_nat_at = {2 + k: k for k in range(HC)}
        wdt_load_q = list(range(IC))
        wdt_comp_q = list(range(IC))
        wdt_inflight = []
        for ic in range(IC):
            for w in range(NW):
                if ic + 3 < IC:
                    if w == 1:
                        w_chunk_s2a("g", ic + 3)
                    elif w == 2:
                        w_chunk_s2a("u", ic + 3)
                    elif w == 3:
                        w_chunk_s2b("g", ic + 3)
                    elif w == 4:
                        w_chunk_s3("g", ic + 3)
                    elif w == 5:
                        w_chunk_s2b("u", ic + 3)
                    elif w == 6:
                        w_chunk_s3("u", ic + 3)
                emit_sweep(ic, w)
                if ic == 0 and w < NW - 2:
                    for t in range(4 * (w + 2), 4 * (w + 3)):
                        emit_xq_tile(t)
                if ic + 3 < IC and w == 0:
                    w_chunk_s1("g", ic + 3)
                    w_chunk_s1("u", ic + 3)
                if ic in down_nat_at:
                    if w in (1, 3, 5):
                        wd_nat_piece(down_nat_at[ic], (w - 1) // 2)
                    elif w == 7:
                        wd_nat_finish(down_nat_at[ic])
                if ic >= 10:
                    if w % 2 == 0 and wdt_load_q and len(wdt_inflight) < 2:
                        icc = wdt_load_q.pop(0)
                        emit_wd_t_load(icc)
                        wdt_inflight.append(icc)
                    elif w % 2 == 1 and (len(wdt_inflight) == 2 or not wdt_load_q) and wdt_inflight:
                        icc = wdt_inflight.pop(0)
                        assert icc == wdt_comp_q.pop(0)
                        emit_wd_t_compute(icc)
        E("scalar", lambda h: h.dma_start(out=sd_d, in_=s_d[:]), reads=bs_d, writes=[bsd_d], dma=True)
        E("vector", lambda h: h.reduce_max(out=acol[:], in_=slots2[:], axis=AX.X), reads=bslot2, writes=[bacol])
        E("scalar", lambda h: h.dma_start(out=bounce.ap().transpose([1, 0]), in_=acol[:]), reads=[bacol], writes=[bbounce], dma=True)
        E("sync", lambda h: h.dma_start(out=arow[:], in_=bounce.ap()), reads=[bbounce], writes=[barow], dma=True)
        E("vector", lambda h: h.reduce_max(out=aout[:], in_=arow[:], axis=AX.X), reads=[barow], writes=[baout])
        E("scalar", lambda h: h.dma_start(out=amax2_d, in_=aout[:]), reads=[baout], writes=[bamax2_d], dma=True)

        block = ctx.enter_context(nc.Block())
        pr.replay(nc, block, ctx)
    return nc



def build_l3():
    nc = bass.Bass("TRN2", target_bir_lowering=False)
    g_d = nc.dram_tensor("g_in", [NT, P, IC, P], F32, kind="ExternalInput").ap()
    xres_d = nc.dram_tensor("xres3_in", [S, D], F32, kind="ExternalInput").ap()
    mdt_d = nc.dram_tensor("mdt_in", [IC, P, D], BF16, kind="ExternalInput").ap()
    am_d = nc.dram_tensor("amax2_all", [1, B], F32, kind="ExternalInput").ap()
    clip_d = nc.dram_tensor("clip_down_in", [1, 1], F32, kind="ExternalInput").ap()
    out_d = nc.dram_tensor("out", [S, D], F32, kind="ExternalOutput").ap()
    eye_h = nc.inline_tensor(np.eye(P, dtype=np.float32), "eye3_f32_h")

    with ExitStack() as ctx:
        sb = lambda n, s, d: ctx.enter_context(nc.sbuf_tensor(n, s, d))
        ps = lambda n, s, d: ctx.enter_context(nc.psum_tensor(n, s, d))
        NB3 = 3
        NG = 4
        NX = 6
        mdT = sb("mdT", [P, IC * D], BF16)
        gin = [sb(f"gin{i}", [P, I], F32) for i in range(NG)]
        n2i = [sb(f"n2i{i}", [P, I], I8) for i in range(NB3)]
        n2b = [sb(f"n2b{i}", [P, I], BF16) for i in range(NB3)]
        xrt = [sb(f"xrt3{i}", [P, D], F32) for i in range(NX)]
        tmp = [sb(f"tmp3{i}", [P, W], F32) for i in range(NB3)]
        y2 = [sb(f"y2_{i}", [P, D], F32) for i in range(NB3)]
        outt = [sb(f"outt{i}", [P, D], F32) for i in range(NB3)]
        stats = [sb(f"st3{i}", [P, 12], F32) for i in range(NB3)]
        mvA = [sb(f"mv3A{i}", [P, 2], F32) for i in range(NB3)]
        mvB = [sb(f"mv3B{i}", [P, 2], F32) for i in range(NB3)]
        mv = [sb(f"mv3{i}", [P, 2], F32) for i in range(NB3)]
        dmq = [sb(f"dmq3{i}", [P, 1], F32) for i in range(NB3)]
        dm2 = [sb(f"dm23_{i}", [P, 1], F32) for i in range(NB3)]
        vs = [sb(f"vs3{i}", [P, 1], F32) for i in range(NB3)]
        sdv = [sb(f"sdv3{i}", [P, 1], F32) for i in range(NB3)]
        rstd = [sb(f"rstd3{i}", [P, 1], F32) for i in range(NB3)]
        nb = [sb(f"nb3{i}", [P, 1], F32) for i in range(NB3)]
        epsc = sb("epsc3", [P, 1], F32)
        eye = sb("eye3", [P, P], F32)
        ones = sb("ones3", [P, P], F32)
        amrow = sb("amrow3", [1, B], F32)
        amg = sb("amg3", [1, 1], F32)
        clip = sb("clip3", [1, 1], F32)
        sig = sb("sig3", [1, 1], F32)
        sh_ = sb("sh3", [1, 1], F32)
        rh = sb("rh3", [1, 1], F32)
        rhbc = sb("rhbc", [P, 1], F32)
        shbc = sb("shbc", [P, 1], F32)
        dps = [ps(f"dps{i}", [P, W], F32) for i in range(2 * NB3)]
        pM = ps("pM3", [P, W], F32)
        pr = Prog()
        E = pr.emit
        bmdT = [Buf() for _ in range(IC)]
        bgin = [Buf() for _ in range(NG)]
        bn2i = [Buf() for _ in range(NB3)]
        bn2b = [Buf() for _ in range(NB3)]
        bxrt = [Buf() for _ in range(NX)]
        btmp = [Buf() for _ in range(NB3)]
        by2 = [Buf() for _ in range(NB3)]
        boutt = [Buf() for _ in range(NB3)]
        bstats = [Buf() for _ in range(NB3)]
        bmvA = [Buf() for _ in range(NB3)]
        bmvB = [Buf() for _ in range(NB3)]
        bmv = [Buf() for _ in range(NB3)]
        bdmq = [Buf() for _ in range(NB3)]
        bdm2 = [Buf() for _ in range(NB3)]
        bvs = [Buf() for _ in range(NB3)]
        bsdv = [Buf() for _ in range(NB3)]
        brstd = [Buf() for _ in range(NB3)]
        bnb = [Buf() for _ in range(NB3)]
        beye = Buf(); bones = Buf()
        bepsc = Buf()
        bamrow = Buf(); bamg = Buf(); bclip = Buf(); bsig = Buf()
        bsh = Buf(); brh = Buf(); brhbc = Buf(); bshbc = Buf()
        bdps = [Buf() for _ in range(2 * NB3)]
        bpM = Buf()
        bout_d = [Buf() for _ in range(NT)]

        E("sync", lambda h: h.dma_start(out=eye[:], in_=eye_h.ap()), writes=[beye], dma=True)
        E("gpsimd", lambda h: h.memset(ones[:], 1.0), writes=[bones])
        E("gpsimd", lambda h: h.memset(epsc[:], EPS), writes=[bepsc])
        E("sync", lambda h: h.dma_start(out=amrow[:], in_=am_d), writes=[bamrow], dma=True)
        E("vector", lambda h: h.reduce_max(out=amg[:], in_=amrow[:], axis=AX.X), reads=[bamrow], writes=[bamg])
        E("sync", lambda h: h.dma_start(out=clip[:], in_=clip_d), writes=[bclip], dma=True)
        E("scalar", lambda h: h.activation(out=sig[:], in_=clip[:], func=AF.Sigmoid), reads=[bclip], writes=[bsig])
        E("vector", lambda h: h.tensor_scalar(out=sh_[:], in0=sig[:], scalar1=amg[:], scalar2=INV127, op0=ALU.mult, op1=ALU.mult), reads=[bsig], sreads=[bamg], writes=[bsh])
        E("vector", lambda h: h.reciprocal(out=rh[:], in_=sh_[:]), reads=[bsh], writes=[brh])
        E("tensor", lambda h: h.matmul(out=pM[:, 0:1], lhsT=ones[0:1, :], rhs=rh[:], start=True, stop=True), reads=[bones, brh], writes=[bpM])
        E("vector", lambda h: h.tensor_copy(out=rhbc[:], in_=pM[:, 0:1]), reads=[bpM], writes=[brhbc])
        E("tensor", lambda h: h.matmul(out=pM[:, 1:2], lhsT=ones[0:1, :], rhs=sh_[:], start=True, stop=True), reads=[bones, bsh], writes=[bpM])
        E("vector", lambda h: h.tensor_copy(out=shbc[:], in_=pM[:, 1:2]), reads=[bpM], writes=[bshbc])
        for icc in range(IC):
            def ldm(h, icc=icc):
                return h.dma_start(out=mdT[:, icc * D:(icc + 1) * D], in_=mdt_d[icc])
            E("sync", ldm, writes=[bmdT[icc]], dma=True, lane=bmdT[icc])

        def stage_A(tt):
            j = tt % NB3
            jg = tt % NG
            jx = tt % NX
            E("sync", lambda h, tt=tt, jg=jg: h.dma_start(out=gin[jg][:], in_=g_d[tt].rearrange("p ic f -> p (ic f)")), writes=[bgin[jg]], dma=True)
            E("scalar", lambda h, j=j, jg=jg: h.activation(out=n2i[j][:], in_=gin[jg][:], func=AF.Copy, scale=rhbc[:]), reads=[bgin[jg], brhbc], writes=[bn2i[j]])
            E("scalar", lambda h, j=j: h.copy(out=n2b[j][:], in_=n2i[j][:]), reads=[bn2i[j]], writes=[bn2b[j]])
            E("sync", lambda h, tt=tt, jx=jx: h.dma_start(out=xrt[jx][:], in_=xres_d[tt * P:(tt + 1) * P, :]), writes=[bxrt[jx]], dma=True)

        def stage_B(tt):
            j = tt % NB3
            for n in range(2):
                pp = 2 * (tt % 3) + n
                for icc in range(IC):
                    def mm(h, j=j, n=n, icc=icc, pp=pp):
                        return h.matmul(out=dps[pp][:], lhsT=n2b[j][:, icc * P:(icc + 1) * P],
                                        rhs=mdT[:, icc * D + n * W: icc * D + n * W + W],
                                        start=(icc == 0), stop=(icc == IC - 1))
                    pr.emit("tensor", mm, reads=[bn2b[j], bmdT[icc]], writes=[bdps[pp]])

        def stage_C(tt):
            j = tt % NB3
            jx = tt % NX
            for n in range(2):
                pp = 2 * (tt % 3) + n
                E("vector", lambda h, j=j, jx=jx, n=n, pp=pp: h.scalar_tensor_tensor(out=y2[j][:, n * W:(n + 1) * W], in0=dps[pp][:], scalar=shbc[:], in1=xrt[jx][:, n * W:(n + 1) * W], op0=ALU.mult, op1=ALU.add), reads=[bdps[pp], bxrt[jx]], sreads=[bshbc], writes=[by2[j]])
            for n in range(2):
                E("vector", lambda h, j=j, n=n: h.bn_stats(out=stats[j][:, n * 6:(n + 1) * 6], in_=y2[j][:, n * W:(n + 1) * W]), reads=[by2[j]], writes=[bstats[j]])
            E("vector", lambda h, j=j: h.bn_aggr(out=mvA[j][:], in_=stats[j][:, 0:6]), reads=[bstats[j]], writes=[bmvA[j]])
            E("vector", lambda h, j=j: h.bn_aggr(out=mvB[j][:], in_=stats[j][:, 6:12]), reads=[bstats[j]], writes=[bmvB[j]])
            E("gpsimd", lambda h, j=j: h.tensor_scalar(out=mv[j][:, 0:1], in0=mvA[j][:, 0:1], scalar1=mvB[j][:, 0:1], scalar2=0.5, op0=ALU.add, op1=ALU.mult), reads=[bmvA[j]], sreads=[bmvB[j]], writes=[bmv[j]])
            E("gpsimd", lambda h, j=j: h.tensor_scalar(out=dmq[j][:], in0=mvA[j][:, 0:1], scalar1=mvB[j][:, 0:1], scalar2=None, op0=ALU.subtract), reads=[bmvA[j]], sreads=[bmvB[j]], writes=[bdmq[j]])
            E("gpsimd", lambda h, j=j: h.tensor_scalar(out=dm2[j][:], in0=dmq[j][:], scalar1=dmq[j][:], scalar2=0.25, op0=ALU.mult, op1=ALU.mult), reads=[bdmq[j]], sreads=[bdmq[j]], writes=[bdm2[j]])
            E("gpsimd", lambda h, j=j: h.tensor_scalar(out=vs[j][:], in0=mvA[j][:, 1:2], scalar1=mvB[j][:, 1:2], scalar2=0.5, op0=ALU.add, op1=ALU.mult), reads=[bmvA[j]], sreads=[bmvB[j]], writes=[bvs[j]])
            E("gpsimd", lambda h, j=j: h.tensor_add(out=mv[j][:, 1:2], in0=vs[j][:], in1=dm2[j][:]), reads=[bdm2[j], bvs[j]], writes=[bmv[j]])
            E("scalar", lambda h, j=j: h.activation(out=sdv[j][:], in_=mv[j][:, 1:2], func=AF.Sqrt, bias=epsc[:]), reads=[bmv[j], bepsc], writes=[bsdv[j]])
            E("vector", lambda h, j=j: h.reciprocal(out=rstd[j][:], in_=sdv[j][:]), reads=[bsdv[j]], writes=[brstd[j]])
            E("gpsimd", lambda h, j=j: h.tensor_scalar(out=nb[j][:], in0=mv[j][:, 0:1], scalar1=rstd[j][:], scalar2=-1.0, op0=ALU.mult, op1=ALU.mult), reads=[bmv[j]], sreads=[brstd[j]], writes=[bnb[j]])
            E("scalar", lambda h, j=j: h.activation(out=outt[j][:], in_=y2[j][:], func=AF.Identity, scale=rstd[j][:], bias=nb[j][:]), reads=[by2[j], brstd[j], bnb[j]], writes=[boutt[j]])
            E("scalar", lambda h, tt=tt, j=j: h.dma_start(out=out_d[tt * P:(tt + 1) * P, :], in_=outt[j][:]), reads=[boutt[j]], writes=[bout_d[tt]], dma=True, lane=boutt[j])

        LAG = 2
        for t0 in range(LAG):
            stage_A(t0)
        for tt in range(NT):
            if tt + LAG < NT:
                stage_A(tt + LAG)
            stage_B(tt)
            stage_C(tt)

        block = ctx.enter_context(nc.Block())
        pr.replay(nc, block, ctx)
    return nc


_CACHE = {}


def _get(name, builder):
    if name not in _CACHE:
        _CACHE[name] = builder()
    return _CACHE[name]


def kernel(L0_attn_x, L0_attn_filt, L0_ffn_clip_x, L0_ffn_clip_down,
           L0_ffn_w_gate, L0_ffn_w_up, L0_ffn_w_down):
    x = np.ascontiguousarray(np.asarray(L0_attn_x, dtype=np.float32))
    filt = np.asarray(L0_attn_filt, dtype=np.float32)
    clip_x = np.asarray(L0_ffn_clip_x, dtype=np.float32).reshape(1, 1)
    clip_down = np.asarray(L0_ffn_clip_down, dtype=np.float32).reshape(1, 1)
    wg = np.asarray(L0_ffn_w_gate, dtype=np.float32)
    wu = np.asarray(L0_ffn_w_up, dtype=np.float32)
    wd = np.asarray(L0_ffn_w_down, dtype=np.float32)

    gpad = np.concatenate([np.zeros(P, np.float32), filt[::-1], np.zeros(P, np.float32)])
    sw = np.lib.stride_tricks.sliding_window_view(gpad, P)
    cc, uu = np.meshgrid(np.arange(FC), np.arange(P), indexing="ij")
    F_host = np.ascontiguousarray(sw[2175 - 128 * cc - uu], dtype=np.float32)

    wg_nat = np.ascontiguousarray(wg.reshape(IC, P, D))
    wu_nat = np.ascontiguousarray(wu.reshape(IC, P, D))
    wd_nat = np.ascontiguousarray(wd.reshape(HC, P, I))
    wd_t = np.ascontiguousarray(wd.T.reshape(IC, P, D))

    nc1 = _get("l1", build_l1)
    in1 = [dict(x_in=np.ascontiguousarray(x[c]), f_in=F_host) for c in range(B)]
    r1 = run_bass_kernel_spmd(nc1, in1, core_ids=list(range(B)))
    amax1_all = np.concatenate([np.asarray(r1.results[c]["amax1"]) for c in range(B)], axis=1)
    xres = [np.asarray(r1.results[c]["xres"]) for c in range(B)]

    nc2 = _get("l2", build_l2)
    in2 = [dict(xres_in=xres[c], amax1_all=amax1_all, clip_x_in=clip_x,
                wg_nat=wg_nat, wu_nat=wu_nat,
                wd_nat=wd_nat, wd_t=wd_t) for c in range(B)]
    r2 = run_bass_kernel_spmd(nc2, in2, core_ids=list(range(B)))
    amax2_all = np.concatenate([np.asarray(r2.results[c]["amax2"]) for c in range(B)], axis=1)

    nc3 = _get("l3", build_l3)
    in3 = [dict(g_in=np.asarray(r2.results[c]["g_out"]),
                xres3_in=xres[c],
                mdt_in=np.asarray(r2.results[c]["mdt_out"]),
                amax2_all=amax2_all, clip_down_in=clip_down) for c in range(B)]
    r3 = run_bass_kernel_spmd(nc3, in3, core_ids=list(range(B)))
    out = np.stack([np.asarray(r3.results[c]["out"]) for c in range(B)], axis=0)
    return out.astype(np.float32, copy=False)
